# version 05
# baseline (speedup 1.0000x reference)
import math
import numpy as np
import concourse.bass as bass
import concourse.mybir as mybir
from concourse.bass_utils import run_bass_kernel_spmd

F32 = mybir.dt.float32
F16 = mybir.dt.float16
BF16 = mybir.dt.bfloat16
I32 = mybir.dt.int32
AF = mybir.ActivationFunctionType
ALU = mybir.AluOpType

B, S, E = 4, 2048, 4096
H, KVH, D = 32, 8, 128
GRP = H // KVH
HID = 11008
NSB = S // 128
QSB = NSB // 2
QTOK = QSB * 128
NUM_REMAIN = 410
NEGC = -3000.0
EC = E // 128
OB = HID // 128
O_PER_RANK = 11
ORR = O_PER_RANK * 128
SCB = 64
NF4 = [-1.0, -0.6961928009986877, -0.5250730514526367, -0.39491748809814453,
       -0.28444138169288635, -0.18477343022823334, -0.09105003625154495, 0.0,
       0.07958029955625534, 0.16093020141124725, 0.24611230194568634,
       0.33791524171829224, 0.44070982933044434, 0.5626170039176941,
       0.7229568362236023, 1.0]
DELTAS = [NF4[k] - NF4[k - 1] for k in range(1, 16)]
INV_SQRT_D = 1.0 / math.sqrt(D)

PAIR_SLOTS = [[0, 15, 2, 13, 4, 11, 6, 9], [1, 14, 3, 12, 5, 10, 7, 8]]
SLOT_JB = [max(a, b) + 1 for a, b in zip(PAIR_SLOTS[0], PAIR_SLOTS[1])]
SLOT_OFF = [0]
for _jb in SLOT_JB:
    SLOT_OFF.append(SLOT_OFF[-1] + _jb * 128)
SIMW = SLOT_OFF[-1]


def _o_rank_of_block(g):
    if g < 66:
        return g // 11, g % 11
    g -= 66
    return 6 + g // 10, g % 10


def ceil_div(a, b):
    return (a + b - 1) // b


class Sched:

    ENGS = ("sync", "tensor", "vector", "scalar", "gpsimd")
    NPOOL = 8

    def __init__(self):
        self.ops = {e: [] for e in self.ENGS}
        self.cnt = {}
        self.sems = {}
        for e in ("e_tensor", "e_vector", "e_scalar", "cc"):
            self.cnt[e] = 0
        self.pool = {}
        self.cur = {}
        for q in ("sync", "gpsimd"):
            self.pool[q] = [f"q_{q}{i}" for i in range(self.NPOOL)]
            for n in self.pool[q]:
                self.cnt[n] = 0
            self.cur[q] = 0

    def stream(self, q):
        self.cur[q] = (self.cur[q] + 1) % self.NPOOL

    def on(self, eng, fn):
        self.ops[eng].append(fn)

    def inc(self, name, amt):
        self.cnt[name] += amt
        return self.cnt[name]

    def wait(self, eng, name, val=None):
        if isinstance(name, tuple):
            name, val = name
        v = self.cnt[name] if val is None else val
        if v <= 0:
            return
        self.ops[eng].append(lambda e, n=name, vv=v: e.wait_ge(self.sems[n], vv))

    def dma(self, q, out, in_):
        sem = self.pool[q][self.cur[q]]
        self.inc(sem, 16)
        self.ops[q].append(lambda e, o=out, i=in_, s_=sem:
                           e.dma_start(out=o, in_=i).then_inc(self.sems[s_], 16))
        return (sem, self.cnt[sem])

    def emit(self, eng, builder, inc=False):
        sem = "e_" + eng
        tgt = self.inc(sem, 1) if inc else None
        def fn(e, b=builder, t=tgt, s_=sem):
            i = b(e)
            if t is not None:
                i.then_inc(self.sems[s_], 1)
        self.on(eng, fn)
        return (sem, self.cnt[sem])

    def mark(self, name):
        self.marks = getattr(self, "marks", {})
        self.marks[name] = ({e: len(self.ops[e]) for e in self.ENGS}, dict(self.cnt))

    def truncate(self, name):
        lens, cnt = self.marks[name]
        for e in self.ENGS:
            self.ops[e] = self.ops[e][:lens[e]]
        self.cnt = dict(cnt)


def build(dbg=(), stop=None, dbg3=None):
    nc = bass.Bass("TRN2", target_bir_lowering=False, debug=False)
    s = Sched()

    ap = {}
    def ein(name, shape, dt):
        ap[name] = nc.dram_tensor(name, shape, dt, kind="ExternalInput").ap()
    ein("x_full", [S, E], F32)
    ein("x_q", [QTOK, E], F32)
    ein("Wq", [E, E], F32)
    ein("Wk", [KVH * D, E], F32)
    ein("Wv", [KVH * D, E], F32)
    ein("Wo", [E, E], F32)
    ein("p1", [H, D, D], F32)
    ein("b1", [D, H], F32)
    ein("p2", [H, D, D], F32)
    ein("b2", [D, H], F32)
    ein("cos_q", [D, QTOK], F32)
    ein("sin_q", [D, QTOK], F32)
    ein("cos_k", [D, S], F32)
    ein("sin_k", [D, S], F32)
    ein("causal", [QSB, 128, S], BF16)
    ein("ident", [128, 128], F32)
    ein("w1q_sh", [ORR, E], I32)
    ein("w1s_sh", [ORR, E // SCB], F32)
    ein("w2q_sh", [ORR, E], I32)
    ein("w2s_sh", [ORR, E // SCB], F32)
    ein("w3q_sh", [E, ORR], I32)
    ein("w3s_sh", [E, ORR // SCB], F32)

    y = nc.dram_tensor("y", [QTOK, E], F32, kind="ExternalOutput").ap()

    dr = {}
    def scr(name, shape, dt):
        dr[name] = nc.dram_tensor(name, shape, dt, kind="Internal")
    scr("xT_d", [E, S], BF16)
    scr("xqT_d", [E, QTOK], BF16)
    scr("wkT_d", [E, KVH * D], BF16)
    scr("wvT_d", [E, KVH * D], BF16)
    scr("woT_d", [E, E], BF16)
    scr("krT_d", [KVH * D, S], BF16)
    scr("v_d", [KVH * 128, NSB * 128], BF16)
    scr("attn_d", [E, QTOK], BF16)
    scr("x2_d", [QTOK, E], BF16)
    scr("ffo_d", [QTOK, E], F32)
    scr("w1T_sh_d", [ORR, E], BF16)
    scr("w2T_sh_d", [ORR, E], BF16)
    scr("w3T_sh_d", [ORR, E], BF16)
    scr("g1_d", [8 * ORR, E], BF16)
    scr("g2_d", [8 * ORR, E], BF16)
    scr("g3_d", [8 * ORR, E], BF16)

    dbg_outs = {}
    dbg_dtypes = {"xT_d": BF16, "xqT_d": BF16, "wkT_d": BF16, "wvT_d": BF16,
                  "woT_d": BF16, "krT_d": BF16, "v_d": BF16, "attn_d": BF16,
                  "x2_d": BF16, "ffo_d": F32, "w1T_sh_d": BF16, "w2T_sh_d": BF16,
                  "w3T_sh_d": BF16, "g1_d": BF16, "g2_d": BF16, "g3_d": BF16}
    for name in dbg:
        shp = [int(v) for v in dr[name].ap().shape]
        dbg_outs[name] = nc.dram_tensor("dbg_" + name, shp, dbg_dtypes[name],
                                        kind="ExternalOutput").ap()
    d3o = {}
    if dbg3 is not None:
        for nm, w, dt3 in [("sim", S, BF16), ("dge", S, BF16), ("deq", S, F16),
                           ("dP", S, F16), ("keep", S, BF16), ("E", S, BF16),
                           ("PnT", S, BF16), ("vg", S, BF16), ("krg", S, BF16),
                           ("attnout", 128, BF16), ("lo", QSB, F32), ("hi", QSB, F32),
                           ("chi", QSB, F32), ("cnt", QSB, F32), ("m", QSB, F32),
                           ("rs", 1, F32), ("rc", 1, F32)]:
            d3o[nm] = nc.dram_tensor("dbg3_" + nm, [128, w], dt3,
                                     kind="ExternalOutput").ap()

    MBASE = 17408

    class Region:
        def __init__(self, base):
            self.cur = base
        def take(self, nbytes):
            off = self.cur
            self.cur += (nbytes + 63) & ~63
            return off

    def nbytes(shape, dt):
        n = 1
        for v in shape[1:]:
            n *= v
        return n * {F32: 4, F16: 2, BF16: 2, I32: 4}[dt]

    LAY = {}
    def lay(region, name, shape, dt):
        LAY[name] = (region.take(nbytes(shape, dt)), shape, dt)

    pers = Region(MBASE)
    lay(pers, "c_ident", [128, 128], BF16)
    lay(pers, "c_b1", [128, H], F32)
    lay(pers, "c_b2", [128, H], F32)
    lay(pers, "ones_col", [128, 1], F32)
    lay(pers, "vflush", [128, 1], F32)
    lay(pers, "sflush", [128, 1], F32)
    trr = Region(pers.cur)
    lay(trr, "tr_nat", [128, 2, E], BF16)
    lay(trr, "tr_stage", [128, 1024], BF16)
    OV = trr.cur

    p0 = Region(OV)
    lay(p0, "dq_codes", [128, E], BF16)
    lay(p0, "dq_u", [128, E], F16)
    lay(p0, "dq_acc", [128, E], F16)
    lay(p0, "dq_w", [128, E], BF16)
    lay(p0, "dq_s", [128, E // SCB], F32)

    p2 = Region(OV)
    lay(p2, "kv_xT", [128, EC * 512], BF16)
    lay(p2, "kv_wc", [128, 2, KVH * 128], BF16)
    lay(p2, "kv_rope", [128, 512], BF16)
    lay(p2, "kv_kr", [128, 512], BF16)
    lay(p2, "kv_t0", [128, 512], BF16)
    lay(p2, "kv_evac", [128, 2, 512], BF16)
    lay(p2, "c_cosk", [128, S], F32)
    lay(p2, "c_sink", [128, S], F32)

    p3 = Region(OV)
    lay(p3, "at_wqnat", [128, E], BF16)
    lay(p3, "at_wqT", [128, E], BF16)
    lay(p3, "at_xq", [128, 2, QTOK], BF16)
    lay(p3, "at_q", [128, QTOK], BF16)
    lay(p3, "at_rope", [128, QTOK], BF16)
    lay(p3, "at_qr", [128, QTOK], BF16)
    lay(p3, "at_p1", [128, 128], BF16)
    lay(p3, "at_p2", [128, 128], BF16)
    lay(p3, "at_s1", [128, S], BF16)
    lay(p3, "at_h1q", [128, QTOK], BF16)
    lay(p3, "at_h1k", [128, S], BF16)
    lay(p3, "at_qh", [128, QTOK], BF16)
    lay(p3, "at_kh", [128, S], BF16)
    lay(p3, "at_krg", [128, S], BF16)
    lay(p3, "at_vg", [128, S], BF16)
    lay(p3, "at_causal", [128, 2, S], BF16)
    lay(p3, "at_sim", [128, SIMW], BF16)
    lay(p3, "at_dump", [128, S], BF16)
    lay(p3, "at_dge", [128, S], BF16)
    lay(p3, "at_deq", [128, S], F16)
    lay(p3, "at_dP", [128, S], F16)
    lay(p3, "at_keep", [128, S], BF16)
    lay(p3, "at_E", [128, S], BF16)
    lay(p3, "at_PnT", [128, S], BF16)
    for nm in ("at_lo", "at_hi", "at_chi", "at_cnt", "at_mid", "at_take", "at_m"):
        lay(p3, nm, [128, QSB], F32)
    lay(p3, "at_rs", [128, 1], F32)
    lay(p3, "at_rc", [128, 1], F32)
    lay(p3, "at_attn", [128, 2, 128], BF16)
    lay(p3, "c_cosq", [128, QTOK], F32)
    lay(p3, "c_sinq", [128, QTOK], F32)

    p4 = Region(OV)
    lay(p4, "wo_lhs", [128, 2, 128], BF16)
    lay(p4, "wo_rhs", [128, 2, E], BF16)
    lay(p4, "wo_x1", [128, E], F32)
    lay(p4, "wo_xorig", [128, 2, E], F32)
    lay(p4, "wo_sq", [128, E], F32)
    for nm in ("ln_sum", "ln_sq", "ln_mean", "ln_var", "ln_rstd", "ln_t0"):
        lay(p4, nm, [128, 1], F32)
    lay(p4, "p4_x2out", [128, E], BF16)

    p5 = Region(OV)
    lay(p5, "ff_x2T", [128, EC * 512], BF16)
    lay(p5, "ff_w1", [128, 2, E], BF16)
    lay(p5, "ff_w2", [128, 2, E], BF16)
    lay(p5, "ff_w3", [128, 2, 512], BF16)
    lay(p5, "ff_silu", [128, 512], BF16)
    lay(p5, "ff_evac", [128, 2, 512], F32)
    lay(p5, "ff_x2row", [128, 2, E], BF16)
    ffh0 = p5.cur
    lay(p5, "ff_h", [128, OB * 512], BF16)
    ln2 = Region(ffh0)
    lay(ln2, "l2_x1", [128, E], F32)
    lay(ln2, "l2_sq", [128, E], F32)
    lay(ln2, "l2_out", [128, E], F32)
    for nm in ("l2_sum", "l2_sq1", "l2_mean", "l2_var", "l2_rstd", "l2_t0"):
        lay(ln2, nm, [128, 1], F32)

    assert max(r.cur for r in (p0, p2, p3, p4, p5)) <= 228000

    ps_specs = [(n, [128, 512], F32) for n in
                ("psA", "psB", "psC", "psD", "psE", "psF", "psG")] + \
               [("psT", [128, 1024], BF16)]
    PSN = ["psA", "psB", "psC", "psD", "psE", "psF", "psG"]

    import contextlib
    stack = contextlib.ExitStack()
    sb = {}
    for name, (off, shape, dt) in LAY.items():
        sb[name] = nc.alloc_sbuf_tensor_at(name, shape, dt, offset=off)
    psb = {}
    for name, shape, dt in ps_specs:
        psb[name] = stack.enter_context(nc.psum_tensor(name, shape, dt))
    for n in list(s.cnt.keys()):
        s.sems[n] = stack.enter_context(nc.semaphore(n))

    def mm(out, lhsT, rhs, start, stop, inc=False):
        return s.emit("tensor", lambda e, o=out, l=lhsT, r=rhs, st=start, sp=stop:
                      e.matmul(o, l, r, start=st, stop=sp), inc=inc)

    def tr(out, in_, inc=False):
        return s.emit("tensor", lambda e, o=out, i=in_:
                      e.transpose(o, i, sb["c_ident"][:]), inc=inc)

    def act(out, in_, func, bias=0.0, scale=1.0, accum=None, inc=False):
        return s.emit("scalar", lambda e, o=out, i=in_, f=func, b=bias, sc=scale,
                      a=accum: e.activation(out=o, in_=i, func=f, bias=b, scale=sc,
                                            accum_out=a), inc=inc)

    def ts(out, in0, s1v, s2v, op0, op1=None, accum=None, inc=False, eng="vector"):
        def fn(e, o=out, i=in0, a=s1v, b=s2v, o0=op0, o1=op1, ac=accum):
            if o1 is None:
                return e.tensor_scalar(out=o, in0=i, scalar1=a, scalar2=b, op0=o0)
            return e.tensor_scalar(out=o, in0=i, scalar1=a, scalar2=b, op0=o0,
                                   op1=o1, accum_out=ac)
        return s.emit(eng, fn, inc=inc)

    def tt(out, in0, in1, op, inc=False, eng="vector"):
        return s.emit(eng, lambda e, o=out, a=in0, b=in1, op_=op:
                      e.tensor_tensor(o, a, b, op_), inc=inc)

    def stt(out, in0, scalar, in1, op0, op1, accum=None, inc=False, eng="vector"):
        return s.emit(eng, lambda e, o=out, a=in0, sc=scalar, b=in1, o0=op0, o1=op1,
                      ac=accum: e.scalar_tensor_tensor(out=o, in0=a, scalar=sc, in1=b,
                                                       op0=o0, op1=o1, accum_out=ac),
                      inc=inc)

    def memset(buf, val, eng="vector", inc=False):
        return s.emit(eng, lambda e, b=buf, v=val: e.memset(b, v), inc=inc)

    def vflush():
        s.emit("vector", lambda e: e.memset(sb["vflush"][:], 0.0), inc=True)

    def vbar():
        vflush()
        s.wait("vector", "e_vector")

    def sflush():
        s.emit("scalar", lambda e: e.activation(out=sb["sflush"][:],
                                                in_=sb["ones_col"][:], func=AF.Copy,
                                                bias=0.0, scale=1.0, accum_out=None),
               inc=True)

    def barrier():
        for eng in Sched.ENGS:
            for semn, v in list(s.cnt.items()):
                if v > 0:
                    s.wait(eng, semn, v)

    s.stream("gpsimd")
    s.dma("gpsimd", sb["c_ident"][:], ap["ident"])
    s.dma("gpsimd", sb["c_b1"][:], ap["b1"])
    s.dma("gpsimd", sb["c_b2"][:], ap["b2"])
    memset(sb["ones_col"][:], 1.0, inc=True)

    def dequant_slab(codes_hbm, scale_hbm, ncols, scale_cols):
        s.wait("gpsimd", "e_vector")
        s.wait("gpsimd", "e_tensor")
        s.stream("gpsimd")
        tok = s.dma("gpsimd", sb["dq_codes"][:, 0:ncols], codes_hbm)
        tok = s.dma("gpsimd", sb["dq_s"][:, 0:scale_cols], scale_hbm)
        s.wait("vector", tok)
        s.wait("vector", "e_tensor")
        memset(sb["dq_acc"][:, 0:ncols], -1.0)
        for k in range(1, 16):
            ts(sb["dq_u"][:, 0:ncols], sb["dq_codes"][:, 0:ncols], float(k), None,
               ALU.is_ge)
            stt(sb["dq_acc"][:, 0:ncols], sb["dq_u"][:, 0:ncols], DELTAS[k - 1],
                sb["dq_acc"][:, 0:ncols], ALU.mult, ALU.add)
        sc_b = sb["dq_s"][:, 0:scale_cols] \
            .rearrange("p (c one) -> p c one", one=1) \
            .to_broadcast([128, scale_cols, SCB])
        acc3 = sb["dq_acc"][:, 0:ncols].rearrange("p (c k) -> p c k", k=SCB)
        w3v = sb["dq_w"][:, 0:ncols].rearrange("p (c k) -> p c k", k=SCB)
        tt(w3v, acc3, sc_b, ALU.mult, inc=True)
        vflush()

    st_tok = [None]

    def transpose_blocks_to(dst_ap_fn, nblocks, src_tile_fn):
        for g in range(ceil_div(nblocks, 8)):
            nblk = min(8, nblocks - g * 8)
            s.wait("tensor", "e_scalar")
            for j in range(nblk):
                tr(psb["psT"][:, j * 128:(j + 1) * 128], src_tile_fn(g * 8 + j),
                   inc=(j == nblk - 1))
            s.wait("scalar", "e_tensor")
            if st_tok[0] is not None:
                s.wait("scalar", st_tok[0])
            act(sb["tr_stage"][:, 0:nblk * 128], psb["psT"][:, 0:nblk * 128], AF.Copy,
                inc=True)
            sflush()
            s.wait("sync", "e_scalar")
            s.stream("sync")
            st_tok[0] = s.dma("sync", dst_ap_fn(g, nblk),
                              sb["tr_stage"][:, 0:nblk * 128])

    for qn, sn, dst in [("w1q_sh", "w1s_sh", "w1T_sh_d"),
                        ("w2q_sh", "w2s_sh", "w2T_sh_d")]:
        for ob in range(O_PER_RANK):
            r0 = ob * 128
            dequant_slab(ap[qn][r0:r0 + 128, :], ap[sn][r0:r0 + 128, :], E, E // SCB)
            s.wait("tensor", "e_vector")
            transpose_blocks_to(
                lambda g, nblk, dst=dst, r0=r0:
                    dr[dst].ap()[r0:r0 + 128, g * 1024:g * 1024 + nblk * 128],
                EC, lambda i: sb["dq_w"][:, i * 128:(i + 1) * 128])
    for eb in range(EC):
        r0 = eb * 128
        dequant_slab(ap["w3q_sh"][r0:r0 + 128, :], ap["w3s_sh"][r0:r0 + 128, :],
                     ORR, ORR // SCB)
        s.wait("tensor", "e_vector")
        transpose_blocks_to(
            lambda g, nblk, r0=r0:
                dr["w3T_sh_d"].ap()[g * 1024:g * 1024 + nblk * 128, r0:r0 + 128]
                .rearrange("(blk p) ee -> p blk ee", p=128),
            O_PER_RANK, lambda i: sb["dq_w"][:, i * 128:(i + 1) * 128])
    s.mark("p0")

    barrier()
    nat_tok = [None, None]

    def transpose_to_scratch(src_ap, nrows, dst_name):
        for rb in range(nrows // 128):
            buf = rb % 2
            if nat_tok[buf] is not None:
                s.wait("gpsimd", "e_tensor", nat_tok[buf])
            s.stream("gpsimd")
            tok = s.dma("gpsimd", sb["tr_nat"][:, buf, :],
                        src_ap[rb * 128:(rb + 1) * 128, :])
            s.wait("tensor", tok)
            transpose_blocks_to(
                lambda g, nblk, rb=rb:
                    dr[dst_name].ap()[g * 1024:g * 1024 + nblk * 128,
                                      rb * 128:(rb + 1) * 128]
                    .rearrange("(blk p) r -> p blk r", p=128),
                EC, lambda i, buf=buf: sb["tr_nat"][:, buf, i * 128:(i + 1) * 128])
            nat_tok[buf] = s.cnt["e_tensor"]

    transpose_to_scratch(ap["x_full"], S, "xT_d")
    transpose_to_scratch(ap["x_q"], QTOK, "xqT_d")
    transpose_to_scratch(ap["Wk"], KVH * D, "wkT_d")
    transpose_to_scratch(ap["Wv"], KVH * D, "wvT_d")
    transpose_to_scratch(ap["Wo"], E, "woT_d")
    s.mark("p1")

    barrier()
    s.stream("gpsimd")
    cos_tok = s.dma("gpsimd", sb["c_cosk"][:], ap["cos_k"])
    cos_tok = s.dma("gpsimd", sb["c_sink"][:], ap["sin_k"])
    s.wait("vector", cos_tok)
    wc_tok = [None, None]
    ev_tok = [None, None]
    for sl in range(S // 512):
        t0 = sl * 512
        s.wait("sync", "e_tensor")
        s.stream("sync")
        for grp4 in range(8):
            xt_tok = s.dma("sync",
                           sb["kv_xT"][:, grp4 * 2048:(grp4 + 1) * 2048]
                           .rearrange("p (ec t) -> p ec t", t=512),
                           dr["xT_d"].ap()[grp4 * 512:(grp4 + 1) * 512, t0:t0 + 512]
                           .rearrange("(ec p) t -> p ec t", p=128))
        s.wait("tensor", xt_tok)
        for pair in range(2):
            for ec in range(EC):
                buf = ec % 2
                if wc_tok[buf] is not None:
                    s.wait("sync", "e_tensor", wc_tok[buf])
                s.stream("sync")
                wtok = s.dma("sync", sb["kv_wc"][:, buf, :],
                             dr["wvT_d"].ap()[ec * 128:(ec + 1) * 128, :])
                s.wait("tensor", wtok)
                for tb in range(2):
                    tbg = pair * 2 + tb
                    for dh in range(2):
                        mm(psb[PSN[tb * 2 + dh]][:],
                           sb["kv_xT"][:, ec * 512 + tbg * 128:ec * 512 + (tbg + 1) * 128],
                           sb["kv_wc"][:, buf, dh * 512:(dh + 1) * 512],
                           start=(ec == 0), stop=(ec == EC - 1),
                           inc=(tb == 1 and dh == 1))
                wc_tok[buf] = s.cnt["e_tensor"]
            s.wait("scalar", "e_tensor")
            for tb in range(2):
                tbg = pair * 2 + tb
                jb = sl * 4 + tbg
                for dh in range(2):
                    buf = (tb * 2 + dh) % 2
                    if ev_tok[buf] is not None:
                        s.wait("scalar", ev_tok[buf])
                    act(sb["kv_evac"][:, buf, :], psb[PSN[tb * 2 + dh]][:], AF.Copy,
                        inc=True)
                    sflush()
                    s.wait("sync", "e_scalar")
                    s.stream("sync")
                    for kl in range(4):
                        kvh_i = dh * 4 + kl
                        ev_tok[buf] = s.dma(
                            "sync",
                            dr["v_d"].ap()[kvh_i * 128:(kvh_i + 1) * 128,
                                           jb * 128:(jb + 1) * 128],
                            sb["kv_evac"][:, buf, kl * 128:(kl + 1) * 128])
            s.wait("tensor", "e_scalar")
        for pair in range(2):
            for ec in range(EC):
                buf = ec % 2
                if wc_tok[buf] is not None:
                    s.wait("sync", "e_tensor", wc_tok[buf])
                s.stream("sync")
                wtok = s.dma("sync", sb["kv_wc"][:, buf, :],
                             dr["wkT_d"].ap()[ec * 128:(ec + 1) * 128, :])
                s.wait("tensor", wtok)
                for kk in range(4):
                    kvh_i = pair * 4 + kk
                    mm(psb[PSN[kk]][:],
                       sb["kv_wc"][:, buf, kvh_i * 128:(kvh_i + 1) * 128],
                       sb["kv_xT"][:, ec * 512:(ec + 1) * 512],
                       start=(ec == 0), stop=(ec == EC - 1), inc=(kk == 3))
                wc_tok[buf] = s.cnt["e_tensor"]
            s.wait("vector", "e_tensor")
            kr_tok = None
            for kk in range(4):
                kvh_i = pair * 4 + kk
                if kr_tok is not None:
                    s.wait("vector", kr_tok)
                ts(sb["kv_kr"][:], psb[PSN[kk]][:], 1.0, None, ALU.mult)
                ts(sb["kv_rope"][0:64, :], sb["kv_kr"][64:128, :], -1.0, None,
                   ALU.mult)
                ts(sb["kv_rope"][64:128, :], sb["kv_kr"][0:64, :], 1.0, None, ALU.mult)
                tt(sb["kv_t0"][:], sb["kv_kr"][:], sb["c_cosk"][:, t0:t0 + 512],
                   ALU.mult)
                tt(sb["kv_rope"][:], sb["kv_rope"][:], sb["c_sink"][:, t0:t0 + 512],
                   ALU.mult)
                tt(sb["kv_kr"][:], sb["kv_t0"][:], sb["kv_rope"][:], ALU.add, inc=True)
                vflush()
                s.wait("sync", "e_vector")
                s.stream("sync")
                kr_tok = s.dma("sync",
                               dr["krT_d"].ap()[kvh_i * 128:(kvh_i + 1) * 128,
                                                t0:t0 + 512],
                               sb["kv_kr"][:])
            s.wait("vector", kr_tok)
            s.wait("tensor", "e_vector")
    barrier()
    for src, dst in [("w1T_sh_d", "g1_d"), ("w2T_sh_d", "g2_d"), ("w3T_sh_d", "g3_d")]:
        s.inc("cc", 1)
        def fn(e, src=src, dst=dst):
            e.collective_compute(
                "AllGather", ALU.bypass,
                replica_groups=[[0, 1, 2, 3, 4, 5, 6, 7]],
                ins=[dr[src].ap().opt()],
                outs=[dr[dst].ap().opt()],
            ).then_inc(s.sems["cc"], 1)
        s.on("gpsimd", fn)
    s.wait("gpsimd", "cc")
    s.mark("p2")

    barrier()
    s.stream("gpsimd")
    cq_tok = s.dma("gpsimd", sb["c_cosq"][:], ap["cos_q"])
    cq_tok = s.dma("gpsimd", sb["c_sinq"][:], ap["sin_q"])
    s.wait("vector", cq_tok)
    xq_tok = [None, None]
    ca_tok = [None, None]
    an_tok = [None, None]
    for h in range(H):
        kv = h // GRP
        s.wait("gpsimd", "e_tensor")
        s.stream("gpsimd")
        wq_tok = s.dma("gpsimd", sb["at_wqnat"][:], ap["Wq"][h * 128:(h + 1) * 128, :])
        wq_tok = s.dma("gpsimd", sb["at_p1"][:], ap["p1"][h, :, :])
        wq_tok = s.dma("gpsimd", sb["at_p2"][:], ap["p2"][h, :, :])
        if h % GRP == 0:
            s.wait("sync", "e_tensor")
            s.stream("sync")
            kvg_tok = s.dma("sync", sb["at_krg"][:],
                            dr["krT_d"].ap()[kv * 128:(kv + 1) * 128, :])
            kvg_tok = s.dma("sync", sb["at_vg"][:],
                            dr["v_d"].ap()[kv * 128:(kv + 1) * 128, :])
            s.wait("tensor", kvg_tok)
            s.wait("vector", kvg_tok)
        s.wait("tensor", wq_tok)
        for g in range(4):
            s.wait("tensor", "e_scalar")
            for j in range(8):
                ec = g * 8 + j
                tr(psb["psT"][:, j * 128:(j + 1) * 128],
                   sb["at_wqnat"][:, ec * 128:(ec + 1) * 128], inc=(j == 7))
            s.wait("scalar", "e_tensor")
            act(sb["at_wqT"][:, g * 1024:(g + 1) * 1024], psb["psT"][:, 0:1024],
                AF.Copy, inc=True)
        s.wait("tensor", "e_scalar")
        for ec in range(EC):
            buf = ec % 2
            if xq_tok[buf] is not None:
                s.wait("sync", "e_tensor", xq_tok[buf])
            s.stream("sync")
            xtok = s.dma("sync", sb["at_xq"][:, buf, :],
                         dr["xqT_d"].ap()[ec * 128:(ec + 1) * 128, :])
            s.wait("tensor", xtok)
            for nh in range(2):
                mm(psb[PSN[nh]][:], sb["at_wqT"][:, ec * 128:(ec + 1) * 128],
                   sb["at_xq"][:, buf, nh * 512:(nh + 1) * 512],
                   start=(ec == 0), stop=(ec == EC - 1), inc=(nh == 1))
            xq_tok[buf] = s.cnt["e_tensor"]
        s.wait("vector", "e_tensor")
        ts(sb["at_q"][:, 0:512], psb["psA"][:], 1.0, None, ALU.mult)
        ts(sb["at_q"][:, 512:1024], psb["psB"][:], 1.0, None, ALU.mult)
        ts(sb["at_rope"][0:64, :], sb["at_q"][64:128, :], -1.0, None, ALU.mult)
        ts(sb["at_rope"][64:128, :], sb["at_q"][0:64, :], 1.0, None, ALU.mult)
        tt(sb["at_qr"][:], sb["at_q"][:], sb["c_cosq"][:], ALU.mult)
        tt(sb["at_rope"][:], sb["at_rope"][:], sb["c_sinq"][:], ALU.mult)
        tt(sb["at_qr"][:], sb["at_qr"][:], sb["at_rope"][:], ALU.add, inc=True)
        vflush()
        s.wait("tensor", "e_vector")
        for nh in range(2):
            mm(psb[PSN[nh]][:], sb["at_p1"][:], sb["at_qr"][:, nh * 512:(nh + 1) * 512],
               start=True, stop=True, inc=(nh == 1))
        s.wait("scalar", "e_tensor")
        s.wait("scalar", "e_vector")
        act(sb["at_s1"][:, 0:512], psb["psA"][:], AF.Silu, bias=sb["c_b1"][:, h:h + 1])
        act(sb["at_s1"][:, 512:1024], psb["psB"][:], AF.Silu,
            bias=sb["c_b1"][:, h:h + 1], inc=True)
        sflush()
        s.wait("vector", "e_scalar")
        tt(sb["at_h1q"][:], sb["at_s1"][:, 0:QTOK], sb["at_qr"][:], ALU.add, inc=True)
        vflush()
        s.wait("tensor", "e_vector")
        for nh in range(2):
            pst = psb[PSN[nh]]
            mm(pst[:], sb["at_p2"][:], sb["at_h1q"][:, nh * 512:(nh + 1) * 512],
               start=True, stop=False)
            mm(pst[:], sb["c_ident"][:], sb["at_h1q"][:, nh * 512:(nh + 1) * 512],
               start=False, stop=True, inc=(nh == 1))
        s.wait("scalar", "e_tensor")
        act(sb["at_qh"][:, 0:512], psb["psA"][:], AF.Sign, bias=sb["c_b2"][:, h:h + 1])
        act(sb["at_qh"][:, 512:1024], psb["psB"][:], AF.Sign,
            bias=sb["c_b2"][:, h:h + 1], inc=True)
        sflush()
        for nh in range(4):
            pst = psb[PSN[2 + nh % 2]]
            if nh >= 2:
                s.wait("tensor", "e_scalar")
            mm(pst[:], sb["at_p1"][:], sb["at_krg"][:, nh * 512:(nh + 1) * 512],
               start=True, stop=True, inc=True)
            s.wait("scalar", "e_tensor")
            act(sb["at_s1"][:, nh * 512:(nh + 1) * 512], pst[:], AF.Silu,
                bias=sb["c_b1"][:, h:h + 1], inc=True)
        sflush()
        s.wait("vector", "e_scalar")
        tt(sb["at_h1k"][:], sb["at_s1"][:], sb["at_krg"][:], ALU.add, inc=True)
        vflush()
        s.wait("tensor", "e_vector")
        for nh in range(4):
            pst = psb[PSN[2 + nh % 2]]
            if nh >= 2:
                s.wait("tensor", "e_scalar")
            mm(pst[:], sb["at_p2"][:], sb["at_h1k"][:, nh * 512:(nh + 1) * 512],
               start=True, stop=False)
            mm(pst[:], sb["c_ident"][:], sb["at_h1k"][:, nh * 512:(nh + 1) * 512],
               start=False, stop=True, inc=True)
            s.wait("scalar", "e_tensor")
            act(sb["at_kh"][:, nh * 512:(nh + 1) * 512], pst[:], AF.Sign,
                bias=sb["c_b2"][:, h:h + 1], inc=True)
        sflush()
        s.wait("tensor", "e_scalar")

        for sbi in range(QSB):
            jext = SLOT_JB[sbi] * 128
            off = SLOT_OFF[sbi]
            cbuf = sbi % 2
            if ca_tok[cbuf] is not None:
                s.wait("gpsimd", "e_vector", ca_tok[cbuf])
            s.stream("gpsimd")
            ctok = s.dma("gpsimd", sb["at_causal"][:, cbuf, 0:jext],
                         ap["causal"][sbi, :, 0:jext])
            qtile = sb["at_qh"][:, sbi * 128:(sbi + 1) * 128]
            for c in range(ceil_div(jext, 512)):
                w = min(512, jext - c * 512)
                pst = psb[PSN[4 + c % 2]]
                s.wait("tensor", "e_vector")
                mm(pst[:, 0:w], qtile, sb["at_kh"][:, c * 512:c * 512 + w],
                   start=True, stop=True, inc=True)
                s.wait("vector", "e_tensor")
                if c == 0:
                    s.wait("vector", ctok)
                tt(sb["at_sim"][:, off + c * 512:off + c * 512 + w], pst[:, 0:w],
                   sb["at_causal"][:, cbuf, c * 512:c * 512 + w], ALU.add, inc=True)
            vflush()
            ca_tok[cbuf] = s.cnt["e_vector"]

        memset(sb["at_lo"][:], -130.0, inc=True)
        vbar()
        for it in range(9):
            step = float(1 << (8 - it))
            ts(sb["at_mid"][:], sb["at_lo"][:], step, None, ALU.add, inc=True)
            vbar()
            for sbi in range(QSB):
                jext = SLOT_JB[sbi] * 128
                off = SLOT_OFF[sbi]
                ts(sb["at_dump"][:, 0:jext], sb["at_sim"][:, off:off + jext],
                   sb["at_mid"][:, sbi:sbi + 1], None, ALU.is_ge, op1=ALU.add,
                   accum=sb["at_cnt"][:, sbi:sbi + 1], inc=(sbi == QSB - 1))
            vbar()
            ts(sb["at_take"][:], sb["at_cnt"][:], float(NUM_REMAIN), None, ALU.is_ge,
               inc=True)
            vbar()
            stt(sb["at_lo"][:], sb["at_take"][:], step, sb["at_lo"][:],
                ALU.mult, ALU.add, inc=True)
            vbar()
        ts(sb["at_hi"][:], sb["at_lo"][:], 1.0, None, ALU.add, inc=True)
        vbar()
        for sbi in range(QSB):
            jext = SLOT_JB[sbi] * 128
            off = SLOT_OFF[sbi]
            ts(sb["at_dump"][:, 0:jext], sb["at_sim"][:, off:off + jext],
               sb["at_hi"][:, sbi:sbi + 1], None, ALU.is_ge, op1=ALU.add,
               accum=sb["at_chi"][:, sbi:sbi + 1], inc=(sbi == QSB - 1))
        vbar()
        ts(sb["at_m"][:], sb["at_chi"][:], -1.0, float(NUM_REMAIN), ALU.mult,
           op1=ALU.add, inc=True)
        vbar()

        for sbi in range(QSB):
            jext = SLOT_JB[sbi] * 128
            njb = SLOT_JB[sbi]
            off = SLOT_OFF[sbi]
            ts(sb["at_deq"][:, 0:jext], sb["at_sim"][:, off:off + jext],
               sb["at_lo"][:, sbi:sbi + 1], None, ALU.is_equal)
            ts(sb["at_dge"][:, 0:jext], sb["at_sim"][:, off:off + jext],
               sb["at_hi"][:, sbi:sbi + 1], None, ALU.is_ge, inc=True)
            vbar()
            s.emit("vector", lambda e, je=jext: e.tensor_tensor_scan(
                out=sb["at_dP"][:, 0:je],
                data0=sb["ones_col"][:].to_broadcast([128, je]),
                data1=sb["at_deq"][:, 0:je], initial=0.0, op0=ALU.mult, op1=ALU.add),
                inc=True)
            vbar()
            stt(sb["at_deq"][:, 0:jext], sb["at_dP"][:, 0:jext],
                sb["at_m"][:, sbi:sbi + 1], sb["at_deq"][:, 0:jext],
                ALU.is_le, ALU.mult, inc=True)
            vbar()
            tt(sb["at_keep"][:, 0:jext], sb["at_dge"][:, 0:jext],
               sb["at_deq"][:, 0:jext], ALU.add, inc=True)
            vbar()
            s.wait("tensor", "e_vector")
            for c in range(ceil_div(jext, 512)):
                w = min(512, jext - c * 512)
                pst = psb[PSN[4 + c % 2]]
                s.wait("tensor", "e_scalar")
                mm(pst[:, 0:w], sb["at_qr"][:, sbi * 128:(sbi + 1) * 128],
                   sb["at_krg"][:, c * 512:c * 512 + w], start=True, stop=True,
                   inc=True)
                s.wait("scalar", "e_tensor")
                act(sb["at_E"][:, c * 512:c * 512 + w], pst[:, 0:w], AF.Exp,
                    scale=INV_SQRT_D, inc=True)
            sflush()
            s.wait("vector", "e_scalar")
            tt(sb["at_E"][:, 0:jext], sb["at_E"][:, 0:jext],
               sb["at_keep"][:, 0:jext], ALU.mult, inc=True)
            vbar()
            ts(sb["at_dump"][:, 0:jext], sb["at_E"][:, 0:jext], 1.0, None, ALU.mult,
               op1=ALU.add, accum=sb["at_rs"][:], inc=True)
            vbar()
            s.emit("vector", lambda e: e.reciprocal(out=sb["at_rc"][:],
                                                    in_=sb["at_rs"][:]), inc=True)
            vbar()
            ts(sb["at_E"][:, 0:jext], sb["at_E"][:, 0:jext], sb["at_rc"][:], None,
               ALU.mult, inc=True)
            vbar()
            if dbg3 is not None and (h, sbi) == dbg3:
                s.stream("sync")
                s.dma("sync", d3o["sim"][:, 0:jext], sb["at_sim"][:, off:off + jext])
                s.dma("sync", d3o["dge"][:, 0:jext], sb["at_dge"][:, 0:jext])
                s.dma("sync", d3o["deq"][:, 0:jext], sb["at_deq"][:, 0:jext])
                s.dma("sync", d3o["dP"][:, 0:jext], sb["at_dP"][:, 0:jext])
                s.dma("sync", d3o["keep"][:, 0:jext], sb["at_keep"][:, 0:jext])
                tok = s.dma("sync", d3o["E"][:, 0:jext], sb["at_E"][:, 0:jext])
                for nm in ("lo", "hi", "chi", "cnt", "m"):
                    s.dma("sync", d3o[nm], sb["at_" + nm][:])
                s.dma("sync", d3o["rs"], sb["at_rs"][:])
                s.dma("sync", d3o["rc"], sb["at_rc"][:])
                s.dma("sync", d3o["vg"], sb["at_vg"][:])
                tok = s.dma("sync", d3o["krg"], sb["at_krg"][:])
                s.wait("vector", tok)
            s.wait("tensor", "e_vector")
            for g in range(ceil_div(njb, 8)):
                nblk = min(8, njb - g * 8)
                s.wait("tensor", "e_scalar")
                for j in range(nblk):
                    jb = g * 8 + j
                    tr(psb["psT"][:, j * 128:(j + 1) * 128],
                       sb["at_E"][:, jb * 128:(jb + 1) * 128], inc=(j == nblk - 1))
                s.wait("scalar", "e_tensor")
                act(sb["at_PnT"][:, g * 1024:g * 1024 + nblk * 128],
                    psb["psT"][:, 0:nblk * 128], AF.Copy, inc=True)
            sflush()
            s.wait("tensor", "e_scalar")
            for jb in range(njb):
                mm(psb["psG"][:, 0:128], sb["at_vg"][:, jb * 128:(jb + 1) * 128],
                   sb["at_PnT"][:, jb * 128:(jb + 1) * 128],
                   start=(jb == 0), stop=(jb == njb - 1), inc=(jb == njb - 1))
            abuf = sbi % 2
            if an_tok[abuf] is not None:
                s.wait("scalar", an_tok[abuf])
            s.wait("scalar", "e_tensor")
            act(sb["at_attn"][:, abuf, :], psb["psG"][:, 0:128], AF.Copy, inc=True)
            sflush()
            s.wait("sync", "e_scalar")
            s.stream("sync")
            an_tok[abuf] = s.dma("sync",
                                 dr["attn_d"].ap()[h * 128:(h + 1) * 128,
                                                   sbi * 128:(sbi + 1) * 128],
                                 sb["at_attn"][:, abuf, :])
            if dbg3 is not None and (h, sbi) == dbg3:
                tok = s.dma("sync", d3o["PnT"][:, 0:jext], sb["at_PnT"][:, 0:jext])
                tok = s.dma("sync", d3o["attnout"], sb["at_attn"][:, abuf, :])
                s.wait("scalar", tok)
            s.wait("tensor", "e_scalar")
    s.mark("p3")

    barrier()

    def emit_ln(x1buf, outbuf, bufs):
        sqb, sumc, sqc, meanc, varc, rstdc, t0c = bufs
        s.wait("vector", "e_scalar")
        ts(sqb, x1buf, 1.0, None, ALU.mult, op1=ALU.add, accum=sumc, inc=True)
        vflush()
        s.wait("scalar", "e_vector")
        act(sqb, x1buf, AF.Square, accum=sqc, inc=True)
        sflush()
        s.wait("vector", "e_scalar")
        ts(meanc, sumc, 1.0 / E, None, ALU.mult, inc=True)
        vbar()
        ts(varc, sqc, 1.0 / E, None, ALU.mult, inc=True)
        vflush()
        s.wait("scalar", "e_vector")
        act(t0c, meanc, AF.Square, inc=True)
        sflush()
        s.wait("vector", "e_scalar")
        stt(varc, t0c, -1.0, varc, ALU.mult, ALU.add, inc=True)
        vbar()
        ts(varc, varc, 1e-5, None, ALU.add, inc=True)
        vflush()
        s.wait("scalar", "e_vector")
        act(t0c, varc, AF.Sqrt, inc=True)
        sflush()
        s.wait("vector", "e_scalar")
        s.emit("vector", lambda e, r=rstdc, t=t0c: e.reciprocal(out=r, in_=t),
               inc=True)
        vbar()
        ts(outbuf, x1buf, meanc, rstdc, ALU.subtract, op1=ALU.mult, inc=True)
        vbar()

    LN1SET = (sb["wo_sq"][:, 0:E], sb["ln_sum"][:], sb["ln_sq"][:], sb["ln_mean"][:],
              sb["ln_var"][:], sb["ln_rstd"][:], sb["ln_t0"][:])
    LN2SET = (sb["l2_sq"][:, 0:E], sb["l2_sum"][:], sb["l2_sq1"][:], sb["l2_mean"][:],
              sb["l2_var"][:], sb["l2_rstd"][:], sb["l2_t0"][:])

    lhs_tok = [None, None]
    xo_tok = [None, None]
    x2st_tok = None
    for sbi in range(QSB):
        xb = sbi % 2
        if xo_tok[xb] is not None:
            s.wait("gpsimd", "e_vector", xo_tok[xb])
        s.stream("gpsimd")
        xotok = s.dma("gpsimd", sb["wo_xorig"][:, xb, :],
                      ap["x_q"][sbi * 128:(sbi + 1) * 128, :])
        for ep in range(2):
            for hd in range(EC):
                buf = hd % 2
                if lhs_tok[buf] is not None:
                    s.wait("sync", "e_tensor", lhs_tok[buf])
                s.stream("sync")
                wtok = s.dma("sync", sb["wo_lhs"][:, buf, :],
                             dr["attn_d"].ap()[hd * 128:(hd + 1) * 128,
                                               sbi * 128:(sbi + 1) * 128])
                wtok = s.dma("sync", sb["wo_rhs"][:, buf, :],
                             dr["woT_d"].ap()[hd * 128:(hd + 1) * 128, :])
                s.wait("tensor", wtok)
                for e4 in range(4):
                    ech = ep * 4 + e4
                    mm(psb[PSN[e4]][:], sb["wo_lhs"][:, buf, :],
                       sb["wo_rhs"][:, buf, ech * 512:(ech + 1) * 512],
                       start=(hd == 0), stop=(hd == EC - 1), inc=(e4 == 3))
                lhs_tok[buf] = s.cnt["e_tensor"]
            s.wait("vector", "e_tensor")
            if ep == 0:
                s.wait("vector", xotok)
            for e4 in range(4):
                ech = ep * 4 + e4
                tt(sb["wo_x1"][:, ech * 512:(ech + 1) * 512], psb[PSN[e4]][:],
                   sb["wo_xorig"][:, xb, ech * 512:(ech + 1) * 512], ALU.add,
                   inc=(e4 == 3))
            vflush()
            s.wait("tensor", "e_vector")
        xo_tok[xb] = s.cnt["e_vector"]
        if x2st_tok is not None:
            s.wait("vector", x2st_tok)
        emit_ln(sb["wo_x1"][:, 0:E], sb["p4_x2out"][:, 0:E], LN1SET)
        s.wait("sync", "e_vector")
        s.stream("sync")
        x2st_tok = s.dma("sync", dr["x2_d"].ap()[sbi * 128:(sbi + 1) * 128, :],
                         sb["p4_x2out"][:, 0:E])
    s.mark("p4")

    barrier()
    w_tok = [None, None]
    w3_tok = [None, None]
    x2_tok = [None, None]
    for ch in range(2):
        t0 = ch * 512
        for tb in range(4):
            xb = tb % 2
            if x2_tok[xb] is not None:
                s.wait("sync", "e_tensor", x2_tok[xb])
            s.stream("sync")
            xtok = s.dma("sync", sb["ff_x2row"][:, xb, 0:E],
                         dr["x2_d"].ap()[t0 + tb * 128:t0 + (tb + 1) * 128, :])
            s.wait("tensor", xtok)
            for g in range(4):
                s.wait("tensor", "e_scalar")
                for j in range(8):
                    ec = g * 8 + j
                    tr(psb["psT"][:, j * 128:(j + 1) * 128],
                       sb["ff_x2row"][:, xb, ec * 128:(ec + 1) * 128], inc=(j == 7))
                s.wait("scalar", "e_tensor")
                for j in range(8):
                    ec = g * 8 + j
                    act(sb["ff_x2T"][:, ec * 512 + tb * 128:ec * 512 + (tb + 1) * 128],
                        psb["psT"][:, j * 128:(j + 1) * 128], AF.Copy, inc=(j == 7))
            x2_tok[xb] = s.cnt["e_tensor"]
        s.wait("tensor", "e_scalar")
        for g in range(OB):
            r, l = _o_rank_of_block(g)
            row0 = r * ORR + l * 128
            buf = g % 2
            if w_tok[buf] is not None:
                s.wait("sync", "e_tensor", w_tok[buf])
            s.stream("sync")
            wtok = s.dma("sync", sb["ff_w1"][:, buf, :],
                         dr["g1_d"].ap()[row0:row0 + 128, :])
            wtok = s.dma("sync", sb["ff_w2"][:, buf, :],
                         dr["g2_d"].ap()[row0:row0 + 128, :])
            s.wait("tensor", wtok)
            for ec in range(EC):
                mm(psb["psA"][:], sb["ff_w1"][:, buf, ec * 128:(ec + 1) * 128],
                   sb["ff_x2T"][:, ec * 512:(ec + 1) * 512],
                   start=(ec == 0), stop=(ec == EC - 1))
                mm(psb["psB"][:], sb["ff_w2"][:, buf, ec * 128:(ec + 1) * 128],
                   sb["ff_x2T"][:, ec * 512:(ec + 1) * 512],
                   start=(ec == 0), stop=(ec == EC - 1), inc=(ec == EC - 1))
            w_tok[buf] = s.cnt["e_tensor"]
            s.wait("scalar", "e_tensor")
            s.wait("scalar", "e_vector")
            act(sb["ff_silu"][:], psb["psA"][:], AF.Silu, inc=True)
            sflush()
            s.wait("vector", "e_scalar")
            stt(sb["ff_h"][:, g * 512:(g + 1) * 512], psb["psB"][:], 1.0,
                sb["ff_silu"][:], ALU.mult, ALU.mult, inc=True)
            vflush()
            s.wait("tensor", "e_vector")
        ev2_tok = [None, None]
        for ech in range(8):
            e0 = ech * 512
            for g in range(OB):
                r, l = _o_rank_of_block(g)
                row0 = r * ORR + l * 128
                buf = g % 2
                if w3_tok[buf] is not None:
                    s.wait("sync", "e_tensor", w3_tok[buf])
                s.stream("sync")
                wtok = s.dma("sync", sb["ff_w3"][:, buf, :],
                             dr["g3_d"].ap()[row0:row0 + 128, e0:e0 + 512])
                s.wait("tensor", wtok)
                for sub in range(4):
                    mm(psb[PSN[sub]][:],
                       sb["ff_h"][:, g * 512 + sub * 128:g * 512 + (sub + 1) * 128],
                       sb["ff_w3"][:, buf, :],
                       start=(g == 0), stop=(g == OB - 1), inc=(sub == 3))
                w3_tok[buf] = s.cnt["e_tensor"]
            s.wait("vector", "e_tensor")
            for sub in range(4):
                vb = sub % 2
                if ev2_tok[vb] is not None:
                    s.wait("vector", ev2_tok[vb])
                ts(sb["ff_evac"][:, vb, :], psb[PSN[sub]][:], 1.0, None, ALU.mult,
                   inc=True)
                vflush()
                s.wait("sync", "e_vector")
                s.stream("sync")
                ev2_tok[vb] = s.dma("sync",
                                    dr["ffo_d"].ap()[t0 + sub * 128:t0 + (sub + 1) * 128,
                                                     e0:e0 + 512],
                                    sb["ff_evac"][:, vb, :])
            s.wait("tensor", "e_vector")
    barrier()
    for tb in range(QSB):
        xb = tb % 2
        s.wait("sync", "e_vector")
        s.stream("sync")
        x2tok = s.dma("sync", sb["ff_x2row"][:, xb, 0:E],
                      dr["x2_d"].ap()[tb * 128:(tb + 1) * 128, :])
        s.wait("gpsimd", "e_vector")
        s.stream("gpsimd")
        fftok = s.dma("gpsimd", sb["l2_x1"][:, 0:E],
                      dr["ffo_d"].ap()[tb * 128:(tb + 1) * 128, :])
        s.wait("vector", x2tok)
        s.wait("vector", fftok)
        tt(sb["l2_x1"][:, 0:E], sb["l2_x1"][:, 0:E], sb["ff_x2row"][:, xb, 0:E],
           ALU.add, inc=True)
        vflush()
        emit_ln(sb["l2_x1"][:, 0:E], sb["l2_out"][:, 0:E], LN2SET)
        s.wait("sync", "e_vector")
        s.stream("sync")
        ytok = s.dma("sync", y[tb * 128:(tb + 1) * 128, :], sb["l2_out"][:, 0:E])
        s.wait("vector", ytok)
    s.mark("p5")
    if stop is not None:
        s.truncate(stop)

    if dbg:
        barrier()
        s.wait("gpsimd", "cc")
        for name in dbg:
            n_r = int(dr[name].ap().shape[0])
            step = max(1, (1 << 22) // max(1, int(dr[name].ap().shape[1])))
            r0 = 0
            s.stream("gpsimd")
            while r0 < n_r:
                r1 = min(n_r, r0 + step)
                s.dma("gpsimd", dbg_outs[name][r0:r1, :], dr[name].ap()[r0:r1, :])
                r0 = r1

    for semn, v in list(s.cnt.items()):
        if v > 0:
            s.wait("gpsimd", semn, v)

    with stack:
        with nc.Block() as block:
            @block.sync
            def _(eng):
                for fn in s.ops["sync"]:
                    fn(eng)

            @block.tensor
            def _(eng):
                for fn in s.ops["tensor"]:
                    fn(eng)

            @block.vector
            def _(eng):
                for fn in s.ops["vector"]:
                    fn(eng)

            @block.scalar
            def _(eng):
                for fn in s.ops["scalar"]:
                    fn(eng)

            @block.gpsimd
            def _(eng):
                for fn in s.ops["gpsimd"]:
                    fn(eng)

    return nc


def _prep_inputs(core, L0_attn_hidden_states, L0_attn_Wq, L0_attn_Wk, L0_attn_Wv,
                 L0_attn_Wo, L0_attn_hash_proj1, L0_attn_hash_bias1,
                 L0_attn_hash_proj2, L0_attn_hash_bias2, L0_ffn_w1_q, L0_ffn_w1_scale,
                 L0_ffn_w2_q, L0_ffn_w2_scale, L0_ffn_w3_q, L0_ffn_w3_scale,
                 consts):
    b, half = core // 2, core % 2
    slots = PAIR_SLOTS[half]
    xb = np.ascontiguousarray(L0_attn_hidden_states[b])
    rows = np.concatenate([np.arange(g * 128, (g + 1) * 128) for g in slots])
    d = {
        "x_full": xb,
        "x_q": np.ascontiguousarray(xb[rows]),
        "Wq": L0_attn_Wq, "Wk": L0_attn_Wk, "Wv": L0_attn_Wv, "Wo": L0_attn_Wo,
        "p1": L0_attn_hash_proj1, "p2": L0_attn_hash_proj2,
        "b1": np.ascontiguousarray(L0_attn_hash_bias1.reshape(H, D).T),
        "b2": np.ascontiguousarray(L0_attn_hash_bias2.reshape(H, D).T),
        "cos_q": np.ascontiguousarray(consts["cos"][:, rows]),
        "sin_q": np.ascontiguousarray(consts["sin"][:, rows]),
        "cos_k": consts["cos"], "sin_k": consts["sin"],
        "causal": consts["causal"][half],
        "ident": consts["ident"],
    }
    r = core
    if r < 6:
        gblk = list(range(11 * r, 11 * r + 11))
    else:
        g0 = 66 + 10 * (r - 6)
        gblk = list(range(g0, g0 + 10)) + [None]
    def rows_of(blks, W, axis, blkrows=128):
        out = []
        for g in blks:
            if g is None:
                shp = list(W.shape)
                shp[axis] = blkrows
                out.append(np.zeros(shp, W.dtype))
            else:
                sl = [slice(None)] * W.ndim
                sl[axis] = slice(g * blkrows, (g + 1) * blkrows)
                out.append(W[tuple(sl)])
        return np.ascontiguousarray(np.concatenate(out, axis=axis))
    d["w1q_sh"] = rows_of(gblk, L0_ffn_w1_q, 0)
    d["w2q_sh"] = rows_of(gblk, L0_ffn_w2_q, 0)
    d["w3q_sh"] = rows_of(gblk, L0_ffn_w3_q, 1)
    d["w1s_sh"] = rows_of(gblk, L0_ffn_w1_scale, 0)
    d["w2s_sh"] = rows_of(gblk, L0_ffn_w2_scale, 0)
    d["w3s_sh"] = rows_of(gblk, L0_ffn_w3_scale, 1, blkrows=2)
    return d


def _make_consts():
    import ml_dtypes
    inv_freq = 1.0 / 10000.0 ** (np.arange(0, D, 2, dtype=np.float64) / D)
    t = np.arange(S, dtype=np.float64)
    freqs = np.outer(t, inv_freq)
    emb = np.concatenate([freqs, freqs], -1)
    cos = np.ascontiguousarray(np.cos(emb).T.astype(np.float32))
    sin = np.ascontiguousarray(np.sin(emb).T.astype(np.float32))
    causal = np.zeros((2, QSB, 128, S), dtype=ml_dtypes.bfloat16)
    jj = np.arange(S)
    for half in range(2):
        for sl, g in enumerate(PAIR_SLOTS[half]):
            srow = g * 128 + np.arange(128)
            m = np.where(jj[None, :] <= srow[:, None], 0.0, NEGC)
            causal[half, sl] = m.astype(ml_dtypes.bfloat16)
    ident = np.eye(128, dtype=np.float32)
    return {"cos": cos, "sin": sin, "causal": causal, "ident": ident}


_NC_CACHE = {}


def kernel(**inputs):
    if "main" not in _NC_CACHE:
        _NC_CACHE["main"] = build()
    nc = _NC_CACHE["main"]
    consts = _make_consts()
    in_maps = [_prep_inputs(c, consts=consts, **inputs) for c in range(8)]
    res = run_bass_kernel_spmd(nc, in_maps, core_ids=list(range(8)))
    out = np.zeros((B, S, E), dtype=np.float32)
    for c in range(8):
        b, half = c // 2, c % 2
        yc = res.results[c]["y"]
        for sl, g in enumerate(PAIR_SLOTS[half]):
            out[b, g * 128:(g + 1) * 128, :] = yc[sl * 128:(sl + 1) * 128, :]
    return out


# version 06
# speedup vs baseline: 1.0096x; 1.0096x over previous
import math
import numpy as np
import concourse.bass as bass
import concourse.mybir as mybir
from concourse.bass_utils import run_bass_kernel_spmd

F32 = mybir.dt.float32
F16 = mybir.dt.float16
BF16 = mybir.dt.bfloat16
I32 = mybir.dt.int32
AF = mybir.ActivationFunctionType
ALU = mybir.AluOpType

B, S, E = 4, 2048, 4096
H, KVH, D = 32, 8, 128
GRP = H // KVH
HID = 11008
NSB = S // 128
QSB = NSB // 2
QTOK = QSB * 128
NUM_REMAIN = 410
NEGC = -3000.0
EC = E // 128
OB = HID // 128
O_PER_RANK = 11
ORR = O_PER_RANK * 128
SCB = 64
NF4 = [-1.0, -0.6961928009986877, -0.5250730514526367, -0.39491748809814453,
       -0.28444138169288635, -0.18477343022823334, -0.09105003625154495, 0.0,
       0.07958029955625534, 0.16093020141124725, 0.24611230194568634,
       0.33791524171829224, 0.44070982933044434, 0.5626170039176941,
       0.7229568362236023, 1.0]
DELTAS = [NF4[k] - NF4[k - 1] for k in range(1, 16)]
INV_SQRT_D = 1.0 / math.sqrt(D)

PAIR_SLOTS = [[0, 15, 2, 13, 4, 11, 6, 9], [1, 14, 3, 12, 5, 10, 7, 8]]
SLOT_JB = [max(a, b) + 1 for a, b in zip(PAIR_SLOTS[0], PAIR_SLOTS[1])]
SLOT_OFF = [0]
for _jb in SLOT_JB:
    SLOT_OFF.append(SLOT_OFF[-1] + _jb * 128)
SIMW = SLOT_OFF[-1]


def _o_rank_of_block(g):
    if g < 66:
        return g // 11, g % 11
    g -= 66
    return 6 + g // 10, g % 10


def ceil_div(a, b):
    return (a + b - 1) // b


class Sched:

    ENGS = ("sync", "tensor", "vector", "scalar", "gpsimd")
    NPOOL = 8

    def __init__(self):
        self.ops = {e: [] for e in self.ENGS}
        self.cnt = {}
        self.sems = {}
        for e in ("e_tensor", "e_vector", "e_scalar", "cc"):
            self.cnt[e] = 0
        self.pool = {}
        self.cur = {}
        for q in ("sync", "gpsimd"):
            self.pool[q] = [f"q_{q}{i}" for i in range(self.NPOOL)]
            for n in self.pool[q]:
                self.cnt[n] = 0
            self.cur[q] = 0

    def stream(self, q):
        self.cur[q] = (self.cur[q] + 1) % self.NPOOL

    def on(self, eng, fn):
        self.ops[eng].append(fn)

    def inc(self, name, amt):
        self.cnt[name] += amt
        return self.cnt[name]

    def wait(self, eng, name, val=None):
        if isinstance(name, tuple):
            name, val = name
        v = self.cnt[name] if val is None else val
        if v <= 0:
            return
        self.ops[eng].append(lambda e, n=name, vv=v: e.wait_ge(self.sems[n], vv))

    def dma(self, q, out, in_):
        sem = self.pool[q][self.cur[q]]
        self.inc(sem, 16)
        self.ops[q].append(lambda e, o=out, i=in_, s_=sem:
                           e.dma_start(out=o, in_=i).then_inc(self.sems[s_], 16))
        return (sem, self.cnt[sem])

    def emit(self, eng, builder, inc=False):
        sem = "e_" + eng
        tgt = self.inc(sem, 1) if inc else None
        def fn(e, b=builder, t=tgt, s_=sem):
            i = b(e)
            if t is not None:
                i.then_inc(self.sems[s_], 1)
        self.on(eng, fn)
        return (sem, self.cnt[sem])

    def mark(self, name):
        self.marks = getattr(self, "marks", {})
        self.marks[name] = ({e: len(self.ops[e]) for e in self.ENGS}, dict(self.cnt))

    def truncate(self, name):
        lens, cnt = self.marks[name]
        for e in self.ENGS:
            self.ops[e] = self.ops[e][:lens[e]]
        self.cnt = dict(cnt)


def build(dbg=(), stop=None, dbg3=None):
    nc = bass.Bass("TRN2", target_bir_lowering=False, debug=False)
    s = Sched()

    ap = {}
    def ein(name, shape, dt):
        ap[name] = nc.dram_tensor(name, shape, dt, kind="ExternalInput").ap()
    ein("x_full", [S, E], F32)
    ein("x_q", [QTOK, E], F32)
    ein("Wq", [E, E], F32)
    ein("Wk", [KVH * D, E], F32)
    ein("Wv", [KVH * D, E], F32)
    ein("Wo", [E, E], F32)
    ein("p1", [H, D, D], F32)
    ein("b1", [D, H], F32)
    ein("p2", [H, D, D], F32)
    ein("b2", [D, H], F32)
    ein("cos_q", [D, QTOK], F32)
    ein("sin_q", [D, QTOK], F32)
    ein("cos_k", [D, S], F32)
    ein("sin_k", [D, S], F32)
    ein("causal", [QSB, 128, S], BF16)
    ein("ident", [128, 128], F32)
    ein("w1q_sh", [ORR, E], I32)
    ein("w1s_sh", [ORR, E // SCB], F32)
    ein("w2q_sh", [ORR, E], I32)
    ein("w2s_sh", [ORR, E // SCB], F32)
    ein("w3q_sh", [E, ORR], I32)
    ein("w3s_sh", [E, ORR // SCB], F32)

    y = nc.dram_tensor("y", [QTOK, E], F32, kind="ExternalOutput").ap()

    dr = {}
    def scr(name, shape, dt):
        dr[name] = nc.dram_tensor(name, shape, dt, kind="Internal")
    scr("xT_d", [E, S], BF16)
    scr("xqT_d", [E, QTOK], BF16)
    scr("wkT_d", [E, KVH * D], BF16)
    scr("wvT_d", [E, KVH * D], BF16)
    scr("woT_d", [E, E], BF16)
    scr("krT_d", [KVH * D, S], BF16)
    scr("v_d", [KVH * 128, NSB * 128], BF16)
    scr("attn_d", [E, QTOK], BF16)
    scr("x2_d", [QTOK, E], BF16)
    scr("ffo_d", [QTOK, E], F32)
    scr("w1T_sh_d", [ORR, E], BF16)
    scr("w2T_sh_d", [ORR, E], BF16)
    scr("w3T_sh_d", [ORR, E], BF16)
    scr("g1_d", [8 * ORR, E], BF16)
    scr("g2_d", [8 * ORR, E], BF16)
    scr("g3_d", [8 * ORR, E], BF16)

    dbg_outs = {}
    dbg_dtypes = {"xT_d": BF16, "xqT_d": BF16, "wkT_d": BF16, "wvT_d": BF16,
                  "woT_d": BF16, "krT_d": BF16, "v_d": BF16, "attn_d": BF16,
                  "x2_d": BF16, "ffo_d": F32, "w1T_sh_d": BF16, "w2T_sh_d": BF16,
                  "w3T_sh_d": BF16, "g1_d": BF16, "g2_d": BF16, "g3_d": BF16}
    for name in dbg:
        shp = [int(v) for v in dr[name].ap().shape]
        dbg_outs[name] = nc.dram_tensor("dbg_" + name, shp, dbg_dtypes[name],
                                        kind="ExternalOutput").ap()
    d3o = {}
    if dbg3 is not None:
        for nm, w, dt3 in [("sim", S, BF16), ("dge", S, BF16), ("deq", S, F16),
                           ("dP", S, F16), ("keep", S, BF16), ("E", S, BF16),
                           ("PnT", S, BF16), ("vg", S, BF16), ("krg", S, BF16),
                           ("attnout", 128, BF16), ("lo", QSB, F32), ("hi", QSB, F32),
                           ("chi", QSB, F32), ("cnt", QSB, F32), ("m", QSB, F32),
                           ("rs", 1, F32), ("rc", 1, F32)]:
            d3o[nm] = nc.dram_tensor("dbg3_" + nm, [128, w], dt3,
                                     kind="ExternalOutput").ap()

    MBASE = 17408

    class Region:
        def __init__(self, base):
            self.cur = base
        def take(self, nbytes):
            off = self.cur
            self.cur += (nbytes + 63) & ~63
            return off

    def nbytes(shape, dt):
        n = 1
        for v in shape[1:]:
            n *= v
        return n * {F32: 4, F16: 2, BF16: 2, I32: 4}[dt]

    LAY = {}
    def lay(region, name, shape, dt):
        LAY[name] = (region.take(nbytes(shape, dt)), shape, dt)

    pers = Region(MBASE)
    lay(pers, "c_ident", [128, 128], BF16)
    lay(pers, "c_b1", [128, H], F32)
    lay(pers, "c_b2", [128, H], F32)
    lay(pers, "ones_col", [128, 1], F32)
    lay(pers, "vflush", [128, 1], F32)
    lay(pers, "sflush", [128, 1], F32)
    trr = Region(pers.cur)
    lay(trr, "tr_nat", [128, 2, E], BF16)
    lay(trr, "tr_stage", [128, 1024], BF16)
    OV = trr.cur

    p0 = Region(OV)
    lay(p0, "dq_codes", [128, E], BF16)
    lay(p0, "dq_u", [128, E], F16)
    lay(p0, "dq_acc", [128, E], F16)
    lay(p0, "dq_w", [128, E], BF16)
    lay(p0, "dq_s", [128, E // SCB], F32)

    p2 = Region(OV)
    lay(p2, "kv_xT", [128, EC * 512], BF16)
    lay(p2, "kv_wc", [128, 2, KVH * 128], BF16)
    lay(p2, "kv_rope", [128, 512], BF16)
    lay(p2, "kv_kr", [128, 512], BF16)
    lay(p2, "kv_t0", [128, 512], BF16)
    lay(p2, "kv_evac", [128, 2, 512], BF16)
    lay(p2, "c_cosk", [128, S], F32)
    lay(p2, "c_sink", [128, S], F32)

    p3 = Region(OV)
    lay(p3, "at_wqnat", [128, 2, E], BF16)
    lay(p3, "at_wqT", [128, E], BF16)
    lay(p3, "at_xq", [128, 2, QTOK], BF16)
    lay(p3, "at_q", [128, QTOK], BF16)
    lay(p3, "at_rope", [128, QTOK], BF16)
    lay(p3, "at_qr", [128, QTOK], BF16)
    lay(p3, "at_p1", [128, 128], BF16)
    lay(p3, "at_p2", [128, 128], BF16)
    lay(p3, "at_s1", [128, S], BF16)
    lay(p3, "at_h1q", [128, QTOK], BF16)
    lay(p3, "at_h1k", [128, S], BF16)
    lay(p3, "at_qh", [128, QTOK], BF16)
    lay(p3, "at_kh", [128, S], BF16)
    lay(p3, "at_krg", [128, S], BF16)
    lay(p3, "at_vg", [128, S], BF16)
    lay(p3, "at_causal", [128, 2, S], BF16)
    lay(p3, "at_sim", [128, SIMW], BF16)
    lay(p3, "at_dump", [128, S], BF16)
    lay(p3, "at_dge", [128, S], BF16)
    lay(p3, "at_deq", [128, S], F16)
    lay(p3, "at_dP", [128, S], F16)
    lay(p3, "at_keep", [128, S], BF16)
    lay(p3, "at_E", [128, S], BF16)
    lay(p3, "at_PnT", [128, S], BF16)
    for nm in ("at_lo", "at_hi", "at_chi", "at_cnt", "at_mid", "at_take", "at_m"):
        lay(p3, nm, [128, QSB], F32)
    lay(p3, "at_rs", [128, 1], F32)
    lay(p3, "at_rc", [128, 1], F32)
    lay(p3, "at_attn", [128, 2, 128], BF16)
    lay(p3, "c_cosq", [128, QTOK], F32)
    lay(p3, "c_sinq", [128, QTOK], F32)

    p4 = Region(OV)
    lay(p4, "wo_lhs", [128, 2, 128], BF16)
    lay(p4, "wo_rhs", [128, 2, E], BF16)
    lay(p4, "wo_x1", [128, E], F32)
    lay(p4, "wo_xorig", [128, 2, E], F32)
    lay(p4, "wo_sq", [128, E], F32)
    for nm in ("ln_sum", "ln_sq", "ln_mean", "ln_var", "ln_rstd", "ln_t0"):
        lay(p4, nm, [128, 1], F32)
    lay(p4, "p4_x2out", [128, E], BF16)

    p5 = Region(OV)
    lay(p5, "ff_x2T", [128, EC * 512], BF16)
    lay(p5, "ff_w1", [128, 2, E], BF16)
    lay(p5, "ff_w2", [128, 2, E], BF16)
    lay(p5, "ff_w3", [128, 2, 512], BF16)
    lay(p5, "ff_silu", [128, 512], BF16)
    lay(p5, "ff_evac", [128, 2, 512], F32)
    lay(p5, "ff_x2row", [128, 2, E], BF16)
    ffh0 = p5.cur
    lay(p5, "ff_h", [128, OB * 512], BF16)
    ln2 = Region(ffh0)
    lay(ln2, "l2_x1", [128, E], F32)
    lay(ln2, "l2_sq", [128, E], F32)
    lay(ln2, "l2_out", [128, E], F32)
    for nm in ("l2_sum", "l2_sq1", "l2_mean", "l2_var", "l2_rstd", "l2_t0"):
        lay(ln2, nm, [128, 1], F32)

    assert max(r.cur for r in (p0, p2, p3, p4, p5)) <= 228000

    ps_specs = [(n, [128, 512], F32) for n in
                ("psA", "psB", "psC", "psD", "psE", "psF", "psG")] + \
               [("psT", [128, 1024], BF16)]
    PSN = ["psA", "psB", "psC", "psD", "psE", "psF", "psG"]

    import contextlib
    stack = contextlib.ExitStack()
    sb = {}
    for name, (off, shape, dt) in LAY.items():
        sb[name] = nc.alloc_sbuf_tensor_at(name, shape, dt, offset=off)
    psb = {}
    for name, shape, dt in ps_specs:
        psb[name] = stack.enter_context(nc.psum_tensor(name, shape, dt))
    for n in list(s.cnt.keys()):
        s.sems[n] = stack.enter_context(nc.semaphore(n))

    def mm(out, lhsT, rhs, start, stop, inc=False):
        return s.emit("tensor", lambda e, o=out, l=lhsT, r=rhs, st=start, sp=stop:
                      e.matmul(o, l, r, start=st, stop=sp), inc=inc)

    def tr(out, in_, inc=False):
        return s.emit("tensor", lambda e, o=out, i=in_:
                      e.transpose(o, i, sb["c_ident"][:]), inc=inc)

    def act(out, in_, func, bias=0.0, scale=1.0, accum=None, inc=False):
        return s.emit("scalar", lambda e, o=out, i=in_, f=func, b=bias, sc=scale,
                      a=accum: e.activation(out=o, in_=i, func=f, bias=b, scale=sc,
                                            accum_out=a), inc=inc)

    def ts(out, in0, s1v, s2v, op0, op1=None, accum=None, inc=False, eng="vector"):
        def fn(e, o=out, i=in0, a=s1v, b=s2v, o0=op0, o1=op1, ac=accum):
            if o1 is None:
                return e.tensor_scalar(out=o, in0=i, scalar1=a, scalar2=b, op0=o0)
            return e.tensor_scalar(out=o, in0=i, scalar1=a, scalar2=b, op0=o0,
                                   op1=o1, accum_out=ac)
        return s.emit(eng, fn, inc=inc)

    def tt(out, in0, in1, op, inc=False, eng="vector"):
        return s.emit(eng, lambda e, o=out, a=in0, b=in1, op_=op:
                      e.tensor_tensor(o, a, b, op_), inc=inc)

    def stt(out, in0, scalar, in1, op0, op1, accum=None, inc=False, eng="vector"):
        return s.emit(eng, lambda e, o=out, a=in0, sc=scalar, b=in1, o0=op0, o1=op1,
                      ac=accum: e.scalar_tensor_tensor(out=o, in0=a, scalar=sc, in1=b,
                                                       op0=o0, op1=o1, accum_out=ac),
                      inc=inc)

    def memset(buf, val, eng="vector", inc=False):
        return s.emit(eng, lambda e, b=buf, v=val: e.memset(b, v), inc=inc)

    def vflush():
        s.emit("vector", lambda e: e.memset(sb["vflush"][:], 0.0), inc=True)

    def vbar():
        vflush()
        s.wait("vector", "e_vector")

    def sflush():
        s.emit("scalar", lambda e: e.activation(out=sb["sflush"][:],
                                                in_=sb["ones_col"][:], func=AF.Copy,
                                                bias=0.0, scale=1.0, accum_out=None),
               inc=True)

    def barrier():
        for eng in Sched.ENGS:
            for semn, v in list(s.cnt.items()):
                if v > 0:
                    s.wait(eng, semn, v)

    s.stream("gpsimd")
    s.dma("gpsimd", sb["c_ident"][:], ap["ident"])
    s.dma("gpsimd", sb["c_b1"][:], ap["b1"])
    s.dma("gpsimd", sb["c_b2"][:], ap["b2"])
    memset(sb["ones_col"][:], 1.0, inc=True)

    def dequant_slab(codes_hbm, scale_hbm, ncols, scale_cols):
        s.wait("gpsimd", "e_vector")
        s.wait("gpsimd", "e_tensor")
        s.stream("gpsimd")
        tok = s.dma("gpsimd", sb["dq_codes"][:, 0:ncols], codes_hbm)
        tok = s.dma("gpsimd", sb["dq_s"][:, 0:scale_cols], scale_hbm)
        s.wait("vector", tok)
        s.wait("vector", "e_tensor")
        memset(sb["dq_acc"][:, 0:ncols], -1.0)
        for k in range(1, 16):
            ts(sb["dq_u"][:, 0:ncols], sb["dq_codes"][:, 0:ncols], float(k), None,
               ALU.is_ge)
            stt(sb["dq_acc"][:, 0:ncols], sb["dq_u"][:, 0:ncols], DELTAS[k - 1],
                sb["dq_acc"][:, 0:ncols], ALU.mult, ALU.add)
        sc_b = sb["dq_s"][:, 0:scale_cols] \
            .rearrange("p (c one) -> p c one", one=1) \
            .to_broadcast([128, scale_cols, SCB])
        acc3 = sb["dq_acc"][:, 0:ncols].rearrange("p (c k) -> p c k", k=SCB)
        w3v = sb["dq_w"][:, 0:ncols].rearrange("p (c k) -> p c k", k=SCB)
        tt(w3v, acc3, sc_b, ALU.mult, inc=True)
        vflush()

    st_tok = [None]

    def transpose_blocks_to(dst_ap_fn, nblocks, src_tile_fn):
        for g in range(ceil_div(nblocks, 8)):
            nblk = min(8, nblocks - g * 8)
            s.wait("tensor", "e_scalar")
            for j in range(nblk):
                tr(psb["psT"][:, j * 128:(j + 1) * 128], src_tile_fn(g * 8 + j),
                   inc=(j == nblk - 1))
            s.wait("scalar", "e_tensor")
            if st_tok[0] is not None:
                s.wait("scalar", st_tok[0])
            act(sb["tr_stage"][:, 0:nblk * 128], psb["psT"][:, 0:nblk * 128], AF.Copy,
                inc=True)
            sflush()
            s.wait("sync", "e_scalar")
            s.stream("sync")
            st_tok[0] = s.dma("sync", dst_ap_fn(g, nblk),
                              sb["tr_stage"][:, 0:nblk * 128])

    for qn, sn, dst in [("w1q_sh", "w1s_sh", "w1T_sh_d"),
                        ("w2q_sh", "w2s_sh", "w2T_sh_d")]:
        for ob in range(O_PER_RANK):
            r0 = ob * 128
            dequant_slab(ap[qn][r0:r0 + 128, :], ap[sn][r0:r0 + 128, :], E, E // SCB)
            s.wait("tensor", "e_vector")
            transpose_blocks_to(
                lambda g, nblk, dst=dst, r0=r0:
                    dr[dst].ap()[r0:r0 + 128, g * 1024:g * 1024 + nblk * 128],
                EC, lambda i: sb["dq_w"][:, i * 128:(i + 1) * 128])
    for eb in range(EC):
        r0 = eb * 128
        dequant_slab(ap["w3q_sh"][r0:r0 + 128, :], ap["w3s_sh"][r0:r0 + 128, :],
                     ORR, ORR // SCB)
        s.wait("tensor", "e_vector")
        transpose_blocks_to(
            lambda g, nblk, r0=r0:
                dr["w3T_sh_d"].ap()[g * 1024:g * 1024 + nblk * 128, r0:r0 + 128]
                .rearrange("(blk p) ee -> p blk ee", p=128),
            O_PER_RANK, lambda i: sb["dq_w"][:, i * 128:(i + 1) * 128])
    s.mark("p0")

    barrier()
    nat_tok = [None, None]

    ld_nat = [None]

    def transpose_to_scratch(src_ap, nrows, dst_name):
        def _ld_nat(rb):
            buf = rb % 2
            if nat_tok[buf] is not None:
                s.wait("gpsimd", "e_tensor", nat_tok[buf])
            s.stream("gpsimd")
            return s.dma("gpsimd", sb["tr_nat"][:, buf, :],
                         src_ap[rb * 128:(rb + 1) * 128, :])
        for rb in range(nrows // 128):
            buf = rb % 2
            if ld_nat[0] is None:
                ld_nat[0] = _ld_nat(rb)
            s.wait("tensor", ld_nat[0])
            ld_nat[0] = _ld_nat(rb + 1) if rb + 1 < nrows // 128 else None
            transpose_blocks_to(
                lambda g, nblk, rb=rb:
                    dr[dst_name].ap()[g * 1024:g * 1024 + nblk * 128,
                                      rb * 128:(rb + 1) * 128]
                    .rearrange("(blk p) r -> p blk r", p=128),
                EC, lambda i, buf=buf: sb["tr_nat"][:, buf, i * 128:(i + 1) * 128])
            nat_tok[buf] = s.cnt["e_tensor"]

    transpose_to_scratch(ap["x_full"], S, "xT_d")
    transpose_to_scratch(ap["x_q"], QTOK, "xqT_d")
    transpose_to_scratch(ap["Wk"], KVH * D, "wkT_d")
    transpose_to_scratch(ap["Wv"], KVH * D, "wvT_d")
    transpose_to_scratch(ap["Wo"], E, "woT_d")
    s.mark("p1")

    barrier()
    s.stream("gpsimd")
    cos_tok = s.dma("gpsimd", sb["c_cosk"][:], ap["cos_k"])
    cos_tok = s.dma("gpsimd", sb["c_sink"][:], ap["sin_k"])
    s.wait("vector", cos_tok)
    wc_tok = [None, None]
    ev_tok = [None, None]
    for sl in range(S // 512):
        t0 = sl * 512
        s.wait("sync", "e_tensor")
        s.stream("sync")
        for grp4 in range(8):
            xt_tok = s.dma("sync",
                           sb["kv_xT"][:, grp4 * 2048:(grp4 + 1) * 2048]
                           .rearrange("p (ec t) -> p ec t", t=512),
                           dr["xT_d"].ap()[grp4 * 512:(grp4 + 1) * 512, t0:t0 + 512]
                           .rearrange("(ec p) t -> p ec t", p=128))
        s.wait("tensor", xt_tok)
        for pair in range(2):
            def _ld_wv(ec):
                buf = ec % 2
                if wc_tok[buf] is not None:
                    s.wait("sync", "e_tensor", wc_tok[buf])
                s.stream("sync")
                return s.dma("sync", sb["kv_wc"][:, buf, :],
                             dr["wvT_d"].ap()[ec * 128:(ec + 1) * 128, :])
            lv_tok = {0: _ld_wv(0)}
            for ec in range(EC):
                buf = ec % 2
                s.wait("tensor", lv_tok[ec])
                if ec + 1 < EC:
                    lv_tok[ec + 1] = _ld_wv(ec + 1)
                for tb in range(2):
                    tbg = pair * 2 + tb
                    for dh in range(2):
                        mm(psb[PSN[tb * 2 + dh]][:],
                           sb["kv_xT"][:, ec * 512 + tbg * 128:ec * 512 + (tbg + 1) * 128],
                           sb["kv_wc"][:, buf, dh * 512:(dh + 1) * 512],
                           start=(ec == 0), stop=(ec == EC - 1),
                           inc=(tb == 1 and dh == 1))
                wc_tok[buf] = s.cnt["e_tensor"]
            s.wait("scalar", "e_tensor")
            for tb in range(2):
                tbg = pair * 2 + tb
                jb = sl * 4 + tbg
                for dh in range(2):
                    buf = (tb * 2 + dh) % 2
                    if ev_tok[buf] is not None:
                        s.wait("scalar", ev_tok[buf])
                    act(sb["kv_evac"][:, buf, :], psb[PSN[tb * 2 + dh]][:], AF.Copy,
                        inc=True)
                    sflush()
                    s.wait("sync", "e_scalar")
                    s.stream("sync")
                    for kl in range(4):
                        kvh_i = dh * 4 + kl
                        ev_tok[buf] = s.dma(
                            "sync",
                            dr["v_d"].ap()[kvh_i * 128:(kvh_i + 1) * 128,
                                           jb * 128:(jb + 1) * 128],
                            sb["kv_evac"][:, buf, kl * 128:(kl + 1) * 128])
            s.wait("tensor", "e_scalar")
        for pair in range(2):
            def _ld_wk(ec):
                buf = ec % 2
                if wc_tok[buf] is not None:
                    s.wait("sync", "e_tensor", wc_tok[buf])
                s.stream("sync")
                return s.dma("sync", sb["kv_wc"][:, buf, :],
                             dr["wkT_d"].ap()[ec * 128:(ec + 1) * 128, :])
            lk_tok = {0: _ld_wk(0)}
            for ec in range(EC):
                buf = ec % 2
                s.wait("tensor", lk_tok[ec])
                if ec + 1 < EC:
                    lk_tok[ec + 1] = _ld_wk(ec + 1)
                for kk in range(4):
                    kvh_i = pair * 4 + kk
                    mm(psb[PSN[kk]][:],
                       sb["kv_wc"][:, buf, kvh_i * 128:(kvh_i + 1) * 128],
                       sb["kv_xT"][:, ec * 512:(ec + 1) * 512],
                       start=(ec == 0), stop=(ec == EC - 1), inc=(kk == 3))
                wc_tok[buf] = s.cnt["e_tensor"]
            s.wait("vector", "e_tensor")
            kr_tok = None
            for kk in range(4):
                kvh_i = pair * 4 + kk
                if kr_tok is not None:
                    s.wait("vector", kr_tok)
                ts(sb["kv_kr"][:], psb[PSN[kk]][:], 1.0, None, ALU.mult)
                ts(sb["kv_rope"][0:64, :], sb["kv_kr"][64:128, :], -1.0, None,
                   ALU.mult)
                ts(sb["kv_rope"][64:128, :], sb["kv_kr"][0:64, :], 1.0, None, ALU.mult)
                tt(sb["kv_t0"][:], sb["kv_kr"][:], sb["c_cosk"][:, t0:t0 + 512],
                   ALU.mult)
                tt(sb["kv_rope"][:], sb["kv_rope"][:], sb["c_sink"][:, t0:t0 + 512],
                   ALU.mult)
                tt(sb["kv_kr"][:], sb["kv_t0"][:], sb["kv_rope"][:], ALU.add, inc=True)
                vflush()
                s.wait("sync", "e_vector")
                s.stream("sync")
                kr_tok = s.dma("sync",
                               dr["krT_d"].ap()[kvh_i * 128:(kvh_i + 1) * 128,
                                                t0:t0 + 512],
                               sb["kv_kr"][:])
            s.wait("vector", kr_tok)
            s.wait("tensor", "e_vector")
    barrier()
    for src, dst in [("w1T_sh_d", "g1_d"), ("w2T_sh_d", "g2_d"), ("w3T_sh_d", "g3_d")]:
        s.inc("cc", 1)
        def fn(e, src=src, dst=dst):
            e.collective_compute(
                "AllGather", ALU.bypass,
                replica_groups=[[0, 1, 2, 3, 4, 5, 6, 7]],
                ins=[dr[src].ap().opt()],
                outs=[dr[dst].ap().opt()],
            ).then_inc(s.sems["cc"], 1)
        s.on("gpsimd", fn)
    s.wait("gpsimd", "cc")
    s.mark("p2")

    barrier()
    s.stream("gpsimd")
    cq_tok = s.dma("gpsimd", sb["c_cosq"][:], ap["cos_q"])
    cq_tok = s.dma("gpsimd", sb["c_sinq"][:], ap["sin_q"])
    s.wait("vector", cq_tok)
    xq_tok = [None, None]
    ca_tok = [None, None]
    an_tok = [None, None]
    wqn_tok = [None, None]

    def _ld_wq(h):
        wb = h % 2
        if wqn_tok[wb] is not None:
            s.wait("gpsimd", "e_tensor", wqn_tok[wb])
        s.stream("gpsimd")
        return s.dma("gpsimd", sb["at_wqnat"][:, wb, :],
                     ap["Wq"][h * 128:(h + 1) * 128, :])
    wql_tok = {0: _ld_wq(0)}
    for h in range(H):
        kv = h // GRP
        wb = h % 2
        s.wait("gpsimd", "e_tensor")
        s.stream("gpsimd")
        wq_tok = s.dma("gpsimd", sb["at_p1"][:], ap["p1"][h, :, :])
        wq_tok = s.dma("gpsimd", sb["at_p2"][:], ap["p2"][h, :, :])
        if h % GRP == 0:
            s.wait("sync", "e_tensor")
            s.stream("sync")
            kvg_tok = s.dma("sync", sb["at_krg"][:],
                            dr["krT_d"].ap()[kv * 128:(kv + 1) * 128, :])
            kvg_tok = s.dma("sync", sb["at_vg"][:],
                            dr["v_d"].ap()[kv * 128:(kv + 1) * 128, :])
            s.wait("tensor", kvg_tok)
            s.wait("vector", kvg_tok)
        s.wait("tensor", wq_tok)
        s.wait("tensor", wql_tok[h])
        if h + 1 < H:
            wql_tok[h + 1] = _ld_wq(h + 1)
        for g in range(4):
            s.wait("tensor", "e_scalar")
            for j in range(8):
                ec = g * 8 + j
                tr(psb["psT"][:, j * 128:(j + 1) * 128],
                   sb["at_wqnat"][:, wb, ec * 128:(ec + 1) * 128], inc=(j == 7))
            s.wait("scalar", "e_tensor")
            act(sb["at_wqT"][:, g * 1024:(g + 1) * 1024], psb["psT"][:, 0:1024],
                AF.Copy, inc=True)
        wqn_tok[wb] = s.cnt["e_tensor"]
        s.wait("tensor", "e_scalar")
        def _ld_xq(ec):
            buf = ec % 2
            if xq_tok[buf] is not None:
                s.wait("sync", "e_tensor", xq_tok[buf])
            s.stream("sync")
            return s.dma("sync", sb["at_xq"][:, buf, :],
                         dr["xqT_d"].ap()[ec * 128:(ec + 1) * 128, :])
        lq_tok = {0: _ld_xq(0)}
        for ec in range(EC):
            buf = ec % 2
            s.wait("tensor", lq_tok[ec])
            if ec + 1 < EC:
                lq_tok[ec + 1] = _ld_xq(ec + 1)
            for nh in range(2):
                mm(psb[PSN[nh]][:], sb["at_wqT"][:, ec * 128:(ec + 1) * 128],
                   sb["at_xq"][:, buf, nh * 512:(nh + 1) * 512],
                   start=(ec == 0), stop=(ec == EC - 1), inc=(nh == 1))
            xq_tok[buf] = s.cnt["e_tensor"]
        s.wait("vector", "e_tensor")
        ts(sb["at_q"][:, 0:512], psb["psA"][:], 1.0, None, ALU.mult)
        ts(sb["at_q"][:, 512:1024], psb["psB"][:], 1.0, None, ALU.mult)
        ts(sb["at_rope"][0:64, :], sb["at_q"][64:128, :], -1.0, None, ALU.mult)
        ts(sb["at_rope"][64:128, :], sb["at_q"][0:64, :], 1.0, None, ALU.mult)
        tt(sb["at_qr"][:], sb["at_q"][:], sb["c_cosq"][:], ALU.mult)
        tt(sb["at_rope"][:], sb["at_rope"][:], sb["c_sinq"][:], ALU.mult)
        tt(sb["at_qr"][:], sb["at_qr"][:], sb["at_rope"][:], ALU.add, inc=True)
        vflush()
        s.wait("tensor", "e_vector")
        for nh in range(2):
            mm(psb[PSN[nh]][:], sb["at_p1"][:], sb["at_qr"][:, nh * 512:(nh + 1) * 512],
               start=True, stop=True, inc=(nh == 1))
        s.wait("scalar", "e_tensor")
        s.wait("scalar", "e_vector")
        act(sb["at_s1"][:, 0:512], psb["psA"][:], AF.Silu, bias=sb["c_b1"][:, h:h + 1])
        act(sb["at_s1"][:, 512:1024], psb["psB"][:], AF.Silu,
            bias=sb["c_b1"][:, h:h + 1], inc=True)
        sflush()
        s.wait("vector", "e_scalar")
        tt(sb["at_h1q"][:], sb["at_s1"][:, 0:QTOK], sb["at_qr"][:], ALU.add, inc=True)
        vflush()
        s.wait("tensor", "e_vector")
        for nh in range(2):
            pst = psb[PSN[nh]]
            mm(pst[:], sb["at_p2"][:], sb["at_h1q"][:, nh * 512:(nh + 1) * 512],
               start=True, stop=False)
            mm(pst[:], sb["c_ident"][:], sb["at_h1q"][:, nh * 512:(nh + 1) * 512],
               start=False, stop=True, inc=(nh == 1))
        s.wait("scalar", "e_tensor")
        act(sb["at_qh"][:, 0:512], psb["psA"][:], AF.Sign, bias=sb["c_b2"][:, h:h + 1])
        act(sb["at_qh"][:, 512:1024], psb["psB"][:], AF.Sign,
            bias=sb["c_b2"][:, h:h + 1], inc=True)
        sflush()
        for nh in range(4):
            pst = psb[PSN[2 + nh % 2]]
            if nh >= 2:
                s.wait("tensor", "e_scalar")
            mm(pst[:], sb["at_p1"][:], sb["at_krg"][:, nh * 512:(nh + 1) * 512],
               start=True, stop=True, inc=True)
            s.wait("scalar", "e_tensor")
            act(sb["at_s1"][:, nh * 512:(nh + 1) * 512], pst[:], AF.Silu,
                bias=sb["c_b1"][:, h:h + 1], inc=True)
        sflush()
        s.wait("vector", "e_scalar")
        tt(sb["at_h1k"][:], sb["at_s1"][:], sb["at_krg"][:], ALU.add, inc=True)
        vflush()
        s.wait("tensor", "e_vector")
        for nh in range(4):
            pst = psb[PSN[2 + nh % 2]]
            if nh >= 2:
                s.wait("tensor", "e_scalar")
            mm(pst[:], sb["at_p2"][:], sb["at_h1k"][:, nh * 512:(nh + 1) * 512],
               start=True, stop=False)
            mm(pst[:], sb["c_ident"][:], sb["at_h1k"][:, nh * 512:(nh + 1) * 512],
               start=False, stop=True, inc=True)
            s.wait("scalar", "e_tensor")
            act(sb["at_kh"][:, nh * 512:(nh + 1) * 512], pst[:], AF.Sign,
                bias=sb["c_b2"][:, h:h + 1], inc=True)
        sflush()
        s.wait("tensor", "e_scalar")

        for sbi in range(QSB):
            jext = SLOT_JB[sbi] * 128
            off = SLOT_OFF[sbi]
            cbuf = sbi % 2
            if ca_tok[cbuf] is not None:
                s.wait("gpsimd", "e_vector", ca_tok[cbuf])
            s.stream("gpsimd")
            ctok = s.dma("gpsimd", sb["at_causal"][:, cbuf, 0:jext],
                         ap["causal"][sbi, :, 0:jext])
            qtile = sb["at_qh"][:, sbi * 128:(sbi + 1) * 128]
            for c in range(ceil_div(jext, 512)):
                w = min(512, jext - c * 512)
                pst = psb[PSN[4 + c % 2]]
                s.wait("tensor", "e_vector")
                mm(pst[:, 0:w], qtile, sb["at_kh"][:, c * 512:c * 512 + w],
                   start=True, stop=True, inc=True)
                s.wait("vector", "e_tensor")
                if c == 0:
                    s.wait("vector", ctok)
                tt(sb["at_sim"][:, off + c * 512:off + c * 512 + w], pst[:, 0:w],
                   sb["at_causal"][:, cbuf, c * 512:c * 512 + w], ALU.add, inc=True)
            vflush()
            ca_tok[cbuf] = s.cnt["e_vector"]

        memset(sb["at_lo"][:], -130.0, inc=True)
        vbar()
        for it in range(9):
            step = float(1 << (8 - it))
            ts(sb["at_mid"][:], sb["at_lo"][:], step, None, ALU.add, inc=True)
            vbar()
            for sbi in range(QSB):
                jext = SLOT_JB[sbi] * 128
                off = SLOT_OFF[sbi]
                ts(sb["at_dump"][:, 0:jext], sb["at_sim"][:, off:off + jext],
                   sb["at_mid"][:, sbi:sbi + 1], None, ALU.is_ge, op1=ALU.add,
                   accum=sb["at_cnt"][:, sbi:sbi + 1], inc=(sbi == QSB - 1))
            vbar()
            ts(sb["at_take"][:], sb["at_cnt"][:], float(NUM_REMAIN), None, ALU.is_ge,
               inc=True)
            vbar()
            stt(sb["at_lo"][:], sb["at_take"][:], step, sb["at_lo"][:],
                ALU.mult, ALU.add, inc=True)
            vbar()
        ts(sb["at_hi"][:], sb["at_lo"][:], 1.0, None, ALU.add, inc=True)
        vbar()
        for sbi in range(QSB):
            jext = SLOT_JB[sbi] * 128
            off = SLOT_OFF[sbi]
            ts(sb["at_dump"][:, 0:jext], sb["at_sim"][:, off:off + jext],
               sb["at_hi"][:, sbi:sbi + 1], None, ALU.is_ge, op1=ALU.add,
               accum=sb["at_chi"][:, sbi:sbi + 1], inc=(sbi == QSB - 1))
        vbar()
        ts(sb["at_m"][:], sb["at_chi"][:], -1.0, float(NUM_REMAIN), ALU.mult,
           op1=ALU.add, inc=True)
        vbar()

        for sbi in range(QSB):
            jext = SLOT_JB[sbi] * 128
            njb = SLOT_JB[sbi]
            off = SLOT_OFF[sbi]
            ts(sb["at_deq"][:, 0:jext], sb["at_sim"][:, off:off + jext],
               sb["at_lo"][:, sbi:sbi + 1], None, ALU.is_equal)
            ts(sb["at_dge"][:, 0:jext], sb["at_sim"][:, off:off + jext],
               sb["at_hi"][:, sbi:sbi + 1], None, ALU.is_ge, inc=True)
            vbar()
            s.emit("vector", lambda e, je=jext: e.tensor_tensor_scan(
                out=sb["at_dP"][:, 0:je],
                data0=sb["ones_col"][:].to_broadcast([128, je]),
                data1=sb["at_deq"][:, 0:je], initial=0.0, op0=ALU.mult, op1=ALU.add),
                inc=True)
            vbar()
            stt(sb["at_deq"][:, 0:jext], sb["at_dP"][:, 0:jext],
                sb["at_m"][:, sbi:sbi + 1], sb["at_deq"][:, 0:jext],
                ALU.is_le, ALU.mult, inc=True)
            vbar()
            tt(sb["at_keep"][:, 0:jext], sb["at_dge"][:, 0:jext],
               sb["at_deq"][:, 0:jext], ALU.add, inc=True)
            vbar()
            s.wait("tensor", "e_vector")
            for c in range(ceil_div(jext, 512)):
                w = min(512, jext - c * 512)
                pst = psb[PSN[4 + c % 2]]
                s.wait("tensor", "e_scalar")
                mm(pst[:, 0:w], sb["at_qr"][:, sbi * 128:(sbi + 1) * 128],
                   sb["at_krg"][:, c * 512:c * 512 + w], start=True, stop=True,
                   inc=True)
                s.wait("scalar", "e_tensor")
                act(sb["at_E"][:, c * 512:c * 512 + w], pst[:, 0:w], AF.Exp,
                    scale=INV_SQRT_D, inc=True)
            sflush()
            s.wait("vector", "e_scalar")
            tt(sb["at_E"][:, 0:jext], sb["at_E"][:, 0:jext],
               sb["at_keep"][:, 0:jext], ALU.mult, inc=True)
            vbar()
            ts(sb["at_dump"][:, 0:jext], sb["at_E"][:, 0:jext], 1.0, None, ALU.mult,
               op1=ALU.add, accum=sb["at_rs"][:], inc=True)
            vbar()
            s.emit("vector", lambda e: e.reciprocal(out=sb["at_rc"][:],
                                                    in_=sb["at_rs"][:]), inc=True)
            vbar()
            ts(sb["at_E"][:, 0:jext], sb["at_E"][:, 0:jext], sb["at_rc"][:], None,
               ALU.mult, inc=True)
            vbar()
            if dbg3 is not None and (h, sbi) == dbg3:
                s.stream("sync")
                s.dma("sync", d3o["sim"][:, 0:jext], sb["at_sim"][:, off:off + jext])
                s.dma("sync", d3o["dge"][:, 0:jext], sb["at_dge"][:, 0:jext])
                s.dma("sync", d3o["deq"][:, 0:jext], sb["at_deq"][:, 0:jext])
                s.dma("sync", d3o["dP"][:, 0:jext], sb["at_dP"][:, 0:jext])
                s.dma("sync", d3o["keep"][:, 0:jext], sb["at_keep"][:, 0:jext])
                tok = s.dma("sync", d3o["E"][:, 0:jext], sb["at_E"][:, 0:jext])
                for nm in ("lo", "hi", "chi", "cnt", "m"):
                    s.dma("sync", d3o[nm], sb["at_" + nm][:])
                s.dma("sync", d3o["rs"], sb["at_rs"][:])
                s.dma("sync", d3o["rc"], sb["at_rc"][:])
                s.dma("sync", d3o["vg"], sb["at_vg"][:])
                tok = s.dma("sync", d3o["krg"], sb["at_krg"][:])
                s.wait("vector", tok)
            s.wait("tensor", "e_vector")
            for g in range(ceil_div(njb, 8)):
                nblk = min(8, njb - g * 8)
                s.wait("tensor", "e_scalar")
                for j in range(nblk):
                    jb = g * 8 + j
                    tr(psb["psT"][:, j * 128:(j + 1) * 128],
                       sb["at_E"][:, jb * 128:(jb + 1) * 128], inc=(j == nblk - 1))
                s.wait("scalar", "e_tensor")
                act(sb["at_PnT"][:, g * 1024:g * 1024 + nblk * 128],
                    psb["psT"][:, 0:nblk * 128], AF.Copy, inc=True)
            sflush()
            s.wait("tensor", "e_scalar")
            for jb in range(njb):
                mm(psb["psG"][:, 0:128], sb["at_vg"][:, jb * 128:(jb + 1) * 128],
                   sb["at_PnT"][:, jb * 128:(jb + 1) * 128],
                   start=(jb == 0), stop=(jb == njb - 1), inc=(jb == njb - 1))
            abuf = sbi % 2
            if an_tok[abuf] is not None:
                s.wait("scalar", an_tok[abuf])
            s.wait("scalar", "e_tensor")
            act(sb["at_attn"][:, abuf, :], psb["psG"][:, 0:128], AF.Copy, inc=True)
            sflush()
            s.wait("sync", "e_scalar")
            s.stream("sync")
            an_tok[abuf] = s.dma("sync",
                                 dr["attn_d"].ap()[h * 128:(h + 1) * 128,
                                                   sbi * 128:(sbi + 1) * 128],
                                 sb["at_attn"][:, abuf, :])
            if dbg3 is not None and (h, sbi) == dbg3:
                tok = s.dma("sync", d3o["PnT"][:, 0:jext], sb["at_PnT"][:, 0:jext])
                tok = s.dma("sync", d3o["attnout"], sb["at_attn"][:, abuf, :])
                s.wait("scalar", tok)
            s.wait("tensor", "e_scalar")
    s.mark("p3")

    barrier()

    def emit_ln(x1buf, outbuf, bufs):
        sqb, sumc, sqc, meanc, varc, rstdc, t0c = bufs
        s.wait("vector", "e_scalar")
        ts(sqb, x1buf, 1.0, None, ALU.mult, op1=ALU.add, accum=sumc, inc=True)
        vflush()
        s.wait("scalar", "e_vector")
        act(sqb, x1buf, AF.Square, accum=sqc, inc=True)
        sflush()
        s.wait("vector", "e_scalar")
        ts(meanc, sumc, 1.0 / E, None, ALU.mult, inc=True)
        vbar()
        ts(varc, sqc, 1.0 / E, None, ALU.mult, inc=True)
        vflush()
        s.wait("scalar", "e_vector")
        act(t0c, meanc, AF.Square, inc=True)
        sflush()
        s.wait("vector", "e_scalar")
        stt(varc, t0c, -1.0, varc, ALU.mult, ALU.add, inc=True)
        vbar()
        ts(varc, varc, 1e-5, None, ALU.add, inc=True)
        vflush()
        s.wait("scalar", "e_vector")
        act(t0c, varc, AF.Sqrt, inc=True)
        sflush()
        s.wait("vector", "e_scalar")
        s.emit("vector", lambda e, r=rstdc, t=t0c: e.reciprocal(out=r, in_=t),
               inc=True)
        vbar()
        ts(outbuf, x1buf, meanc, rstdc, ALU.subtract, op1=ALU.mult, inc=True)
        vbar()

    LN1SET = (sb["wo_sq"][:, 0:E], sb["ln_sum"][:], sb["ln_sq"][:], sb["ln_mean"][:],
              sb["ln_var"][:], sb["ln_rstd"][:], sb["ln_t0"][:])
    LN2SET = (sb["l2_sq"][:, 0:E], sb["l2_sum"][:], sb["l2_sq1"][:], sb["l2_mean"][:],
              sb["l2_var"][:], sb["l2_rstd"][:], sb["l2_t0"][:])

    lhs_tok = [None, None]
    xo_tok = [None, None]
    x2st_tok = None
    for sbi in range(QSB):
        xb = sbi % 2
        if xo_tok[xb] is not None:
            s.wait("gpsimd", "e_vector", xo_tok[xb])
        s.stream("gpsimd")
        xotok = s.dma("gpsimd", sb["wo_xorig"][:, xb, :],
                      ap["x_q"][sbi * 128:(sbi + 1) * 128, :])
        for ep in range(2):
            def _ld_wo(hd, sbi=sbi):
                buf = hd % 2
                if lhs_tok[buf] is not None:
                    s.wait("sync", "e_tensor", lhs_tok[buf])
                s.stream("sync")
                s.dma("sync", sb["wo_lhs"][:, buf, :],
                      dr["attn_d"].ap()[hd * 128:(hd + 1) * 128,
                                        sbi * 128:(sbi + 1) * 128])
                return s.dma("sync", sb["wo_rhs"][:, buf, :],
                             dr["woT_d"].ap()[hd * 128:(hd + 1) * 128, :])
            lw_tok = {0: _ld_wo(0)}
            for hd in range(EC):
                buf = hd % 2
                s.wait("tensor", lw_tok[hd])
                if hd + 1 < EC:
                    lw_tok[hd + 1] = _ld_wo(hd + 1)
                for e4 in range(4):
                    ech = ep * 4 + e4
                    mm(psb[PSN[e4]][:], sb["wo_lhs"][:, buf, :],
                       sb["wo_rhs"][:, buf, ech * 512:(ech + 1) * 512],
                       start=(hd == 0), stop=(hd == EC - 1), inc=(e4 == 3))
                lhs_tok[buf] = s.cnt["e_tensor"]
            s.wait("vector", "e_tensor")
            if ep == 0:
                s.wait("vector", xotok)
            for e4 in range(4):
                ech = ep * 4 + e4
                tt(sb["wo_x1"][:, ech * 512:(ech + 1) * 512], psb[PSN[e4]][:],
                   sb["wo_xorig"][:, xb, ech * 512:(ech + 1) * 512], ALU.add,
                   inc=(e4 == 3))
            vflush()
            s.wait("tensor", "e_vector")
        xo_tok[xb] = s.cnt["e_vector"]
        if x2st_tok is not None:
            s.wait("vector", x2st_tok)
        emit_ln(sb["wo_x1"][:, 0:E], sb["p4_x2out"][:, 0:E], LN1SET)
        s.wait("sync", "e_vector")
        s.stream("sync")
        x2st_tok = s.dma("sync", dr["x2_d"].ap()[sbi * 128:(sbi + 1) * 128, :],
                         sb["p4_x2out"][:, 0:E])
    s.mark("p4")

    barrier()
    w_tok = [None, None]
    w3_tok = [None, None]
    x2_tok = [None, None]
    for ch in range(2):
        t0 = ch * 512
        for tb in range(4):
            xb = tb % 2
            if x2_tok[xb] is not None:
                s.wait("sync", "e_tensor", x2_tok[xb])
            s.stream("sync")
            xtok = s.dma("sync", sb["ff_x2row"][:, xb, 0:E],
                         dr["x2_d"].ap()[t0 + tb * 128:t0 + (tb + 1) * 128, :])
            s.wait("tensor", xtok)
            for g in range(4):
                s.wait("tensor", "e_scalar")
                for j in range(8):
                    ec = g * 8 + j
                    tr(psb["psT"][:, j * 128:(j + 1) * 128],
                       sb["ff_x2row"][:, xb, ec * 128:(ec + 1) * 128], inc=(j == 7))
                s.wait("scalar", "e_tensor")
                for j in range(8):
                    ec = g * 8 + j
                    act(sb["ff_x2T"][:, ec * 512 + tb * 128:ec * 512 + (tb + 1) * 128],
                        psb["psT"][:, j * 128:(j + 1) * 128], AF.Copy, inc=(j == 7))
            x2_tok[xb] = s.cnt["e_tensor"]
        s.wait("tensor", "e_scalar")
        def _ld_w12(g):
            r, l = _o_rank_of_block(g)
            row0 = r * ORR + l * 128
            buf = g % 2
            if w_tok[buf] is not None:
                s.wait("sync", "e_tensor", w_tok[buf])
            s.stream("sync")
            s.dma("sync", sb["ff_w1"][:, buf, :], dr["g1_d"].ap()[row0:row0 + 128, :])
            return s.dma("sync", sb["ff_w2"][:, buf, :],
                         dr["g2_d"].ap()[row0:row0 + 128, :])
        ld_tok = {0: _ld_w12(0)}
        for g in range(OB):
            buf = g % 2
            s.wait("tensor", ld_tok[g])
            if g + 1 < OB:
                ld_tok[g + 1] = _ld_w12(g + 1)
            for ec in range(EC):
                mm(psb["psA"][:], sb["ff_w1"][:, buf, ec * 128:(ec + 1) * 128],
                   sb["ff_x2T"][:, ec * 512:(ec + 1) * 512],
                   start=(ec == 0), stop=(ec == EC - 1))
                mm(psb["psB"][:], sb["ff_w2"][:, buf, ec * 128:(ec + 1) * 128],
                   sb["ff_x2T"][:, ec * 512:(ec + 1) * 512],
                   start=(ec == 0), stop=(ec == EC - 1), inc=(ec == EC - 1))
            w_tok[buf] = s.cnt["e_tensor"]
            s.wait("scalar", "e_tensor")
            s.wait("scalar", "e_vector")
            act(sb["ff_silu"][:], psb["psA"][:], AF.Silu, inc=True)
            sflush()
            s.wait("vector", "e_scalar")
            stt(sb["ff_h"][:, g * 512:(g + 1) * 512], psb["psB"][:], 1.0,
                sb["ff_silu"][:], ALU.mult, ALU.mult, inc=True)
            vflush()
            s.wait("tensor", "e_vector")
        ev2_tok = [None, None]
        for ech in range(8):
            e0 = ech * 512
            def _ld_w3(g, e0=e0):
                r, l = _o_rank_of_block(g)
                row0 = r * ORR + l * 128
                buf = g % 2
                if w3_tok[buf] is not None:
                    s.wait("sync", "e_tensor", w3_tok[buf])
                s.stream("sync")
                return s.dma("sync", sb["ff_w3"][:, buf, :],
                             dr["g3_d"].ap()[row0:row0 + 128, e0:e0 + 512])
            l3_tok = {0: _ld_w3(0)}
            for g in range(OB):
                buf = g % 2
                s.wait("tensor", l3_tok[g])
                if g + 1 < OB:
                    l3_tok[g + 1] = _ld_w3(g + 1)
                for sub in range(4):
                    mm(psb[PSN[sub]][:],
                       sb["ff_h"][:, g * 512 + sub * 128:g * 512 + (sub + 1) * 128],
                       sb["ff_w3"][:, buf, :],
                       start=(g == 0), stop=(g == OB - 1), inc=(sub == 3))
                w3_tok[buf] = s.cnt["e_tensor"]
            s.wait("vector", "e_tensor")
            for sub in range(4):
                vb = sub % 2
                if ev2_tok[vb] is not None:
                    s.wait("vector", ev2_tok[vb])
                ts(sb["ff_evac"][:, vb, :], psb[PSN[sub]][:], 1.0, None, ALU.mult,
                   inc=True)
                vflush()
                s.wait("sync", "e_vector")
                s.stream("sync")
                ev2_tok[vb] = s.dma("sync",
                                    dr["ffo_d"].ap()[t0 + sub * 128:t0 + (sub + 1) * 128,
                                                     e0:e0 + 512],
                                    sb["ff_evac"][:, vb, :])
            s.wait("tensor", "e_vector")
    barrier()
    for tb in range(QSB):
        xb = tb % 2
        s.wait("sync", "e_vector")
        s.stream("sync")
        x2tok = s.dma("sync", sb["ff_x2row"][:, xb, 0:E],
                      dr["x2_d"].ap()[tb * 128:(tb + 1) * 128, :])
        s.wait("gpsimd", "e_vector")
        s.stream("gpsimd")
        fftok = s.dma("gpsimd", sb["l2_x1"][:, 0:E],
                      dr["ffo_d"].ap()[tb * 128:(tb + 1) * 128, :])
        s.wait("vector", x2tok)
        s.wait("vector", fftok)
        tt(sb["l2_x1"][:, 0:E], sb["l2_x1"][:, 0:E], sb["ff_x2row"][:, xb, 0:E],
           ALU.add, inc=True)
        vflush()
        emit_ln(sb["l2_x1"][:, 0:E], sb["l2_out"][:, 0:E], LN2SET)
        s.wait("sync", "e_vector")
        s.stream("sync")
        ytok = s.dma("sync", y[tb * 128:(tb + 1) * 128, :], sb["l2_out"][:, 0:E])
        s.wait("vector", ytok)
    s.mark("p5")
    if stop is not None:
        s.truncate(stop)

    if dbg:
        barrier()
        s.wait("gpsimd", "cc")
        for name in dbg:
            n_r = int(dr[name].ap().shape[0])
            step = max(1, (1 << 22) // max(1, int(dr[name].ap().shape[1])))
            r0 = 0
            s.stream("gpsimd")
            while r0 < n_r:
                r1 = min(n_r, r0 + step)
                s.dma("gpsimd", dbg_outs[name][r0:r1, :], dr[name].ap()[r0:r1, :])
                r0 = r1

    for semn, v in list(s.cnt.items()):
        if v > 0:
            s.wait("gpsimd", semn, v)

    with stack:
        with nc.Block() as block:
            @block.sync
            def _(eng):
                for fn in s.ops["sync"]:
                    fn(eng)

            @block.tensor
            def _(eng):
                for fn in s.ops["tensor"]:
                    fn(eng)

            @block.vector
            def _(eng):
                for fn in s.ops["vector"]:
                    fn(eng)

            @block.scalar
            def _(eng):
                for fn in s.ops["scalar"]:
                    fn(eng)

            @block.gpsimd
            def _(eng):
                for fn in s.ops["gpsimd"]:
                    fn(eng)

    return nc


def _prep_inputs(core, L0_attn_hidden_states, L0_attn_Wq, L0_attn_Wk, L0_attn_Wv,
                 L0_attn_Wo, L0_attn_hash_proj1, L0_attn_hash_bias1,
                 L0_attn_hash_proj2, L0_attn_hash_bias2, L0_ffn_w1_q, L0_ffn_w1_scale,
                 L0_ffn_w2_q, L0_ffn_w2_scale, L0_ffn_w3_q, L0_ffn_w3_scale,
                 consts):
    b, half = core // 2, core % 2
    slots = PAIR_SLOTS[half]
    xb = np.ascontiguousarray(L0_attn_hidden_states[b])
    rows = np.concatenate([np.arange(g * 128, (g + 1) * 128) for g in slots])
    d = {
        "x_full": xb,
        "x_q": np.ascontiguousarray(xb[rows]),
        "Wq": L0_attn_Wq, "Wk": L0_attn_Wk, "Wv": L0_attn_Wv, "Wo": L0_attn_Wo,
        "p1": L0_attn_hash_proj1, "p2": L0_attn_hash_proj2,
        "b1": np.ascontiguousarray(L0_attn_hash_bias1.reshape(H, D).T),
        "b2": np.ascontiguousarray(L0_attn_hash_bias2.reshape(H, D).T),
        "cos_q": np.ascontiguousarray(consts["cos"][:, rows]),
        "sin_q": np.ascontiguousarray(consts["sin"][:, rows]),
        "cos_k": consts["cos"], "sin_k": consts["sin"],
        "causal": consts["causal"][half],
        "ident": consts["ident"],
    }
    r = core
    if r < 6:
        gblk = list(range(11 * r, 11 * r + 11))
    else:
        g0 = 66 + 10 * (r - 6)
        gblk = list(range(g0, g0 + 10)) + [None]
    def rows_of(blks, W, axis, blkrows=128):
        out = []
        for g in blks:
            if g is None:
                shp = list(W.shape)
                shp[axis] = blkrows
                out.append(np.zeros(shp, W.dtype))
            else:
                sl = [slice(None)] * W.ndim
                sl[axis] = slice(g * blkrows, (g + 1) * blkrows)
                out.append(W[tuple(sl)])
        return np.ascontiguousarray(np.concatenate(out, axis=axis))
    d["w1q_sh"] = rows_of(gblk, L0_ffn_w1_q, 0)
    d["w2q_sh"] = rows_of(gblk, L0_ffn_w2_q, 0)
    d["w3q_sh"] = rows_of(gblk, L0_ffn_w3_q, 1)
    d["w1s_sh"] = rows_of(gblk, L0_ffn_w1_scale, 0)
    d["w2s_sh"] = rows_of(gblk, L0_ffn_w2_scale, 0)
    d["w3s_sh"] = rows_of(gblk, L0_ffn_w3_scale, 1, blkrows=2)
    return d


def _make_consts():
    import ml_dtypes
    inv_freq = 1.0 / 10000.0 ** (np.arange(0, D, 2, dtype=np.float64) / D)
    t = np.arange(S, dtype=np.float64)
    freqs = np.outer(t, inv_freq)
    emb = np.concatenate([freqs, freqs], -1)
    cos = np.ascontiguousarray(np.cos(emb).T.astype(np.float32))
    sin = np.ascontiguousarray(np.sin(emb).T.astype(np.float32))
    causal = np.zeros((2, QSB, 128, S), dtype=ml_dtypes.bfloat16)
    jj = np.arange(S)
    for half in range(2):
        for sl, g in enumerate(PAIR_SLOTS[half]):
            srow = g * 128 + np.arange(128)
            m = np.where(jj[None, :] <= srow[:, None], 0.0, NEGC)
            causal[half, sl] = m.astype(ml_dtypes.bfloat16)
    ident = np.eye(128, dtype=np.float32)
    return {"cos": cos, "sin": sin, "causal": causal, "ident": ident}


_NC_CACHE = {}


def kernel(**inputs):
    if "main" not in _NC_CACHE:
        _NC_CACHE["main"] = build()
    nc = _NC_CACHE["main"]
    consts = _make_consts()
    in_maps = [_prep_inputs(c, consts=consts, **inputs) for c in range(8)]
    res = run_bass_kernel_spmd(nc, in_maps, core_ids=list(range(8)))
    out = np.zeros((B, S, E), dtype=np.float32)
    for c in range(8):
        b, half = c // 2, c % 2
        yc = res.results[c]["y"]
        for sl, g in enumerate(PAIR_SLOTS[half]):
            out[b, g * 128:(g + 1) * 128, :] = yc[sl * 128:(sl + 1) * 128, :]
    return out


# version 08
# speedup vs baseline: 1.0714x; 1.0612x over previous
import math
import numpy as np
import concourse.bass as bass
import concourse.mybir as mybir
from concourse.bass_utils import run_bass_kernel_spmd

F32 = mybir.dt.float32
F16 = mybir.dt.float16
BF16 = mybir.dt.bfloat16
I32 = mybir.dt.int32
AF = mybir.ActivationFunctionType
ALU = mybir.AluOpType

B, S, E = 4, 2048, 4096
H, KVH, D = 32, 8, 128
GRP = H // KVH
HID = 11008
NSB = S // 128
QSB = NSB // 2
QTOK = QSB * 128
NUM_REMAIN = 410
NEGC = -3000.0
EC = E // 128
OB = HID // 128
O_PER_RANK = 11
ORR = O_PER_RANK * 128
SCB = 64
NF4 = [-1.0, -0.6961928009986877, -0.5250730514526367, -0.39491748809814453,
       -0.28444138169288635, -0.18477343022823334, -0.09105003625154495, 0.0,
       0.07958029955625534, 0.16093020141124725, 0.24611230194568634,
       0.33791524171829224, 0.44070982933044434, 0.5626170039176941,
       0.7229568362236023, 1.0]
DELTAS = [NF4[k] - NF4[k - 1] for k in range(1, 16)]
INV_SQRT_D = 1.0 / math.sqrt(D)

PAIR_SLOTS = [[0, 15, 2, 13, 4, 11, 6, 9], [1, 14, 3, 12, 5, 10, 7, 8]]
SLOT_JB = [max(a, b) + 1 for a, b in zip(PAIR_SLOTS[0], PAIR_SLOTS[1])]
SLOT_OFF = [0]
for _jb in SLOT_JB:
    SLOT_OFF.append(SLOT_OFF[-1] + _jb * 128)
SIMW = SLOT_OFF[-1]


def _o_rank_of_block(g):
    if g < 66:
        return g // 11, g % 11
    g -= 66
    return 6 + g // 10, g % 10


def ceil_div(a, b):
    return (a + b - 1) // b


class Sched:

    ENGS = ("sync", "tensor", "vector", "scalar", "gpsimd")
    NPOOL = 8

    def __init__(self):
        self.ops = {e: [] for e in self.ENGS}
        self.cnt = {}
        self.sems = {}
        for e in ("e_tensor", "e_vector", "e_scalar", "cc"):
            self.cnt[e] = 0
        self.pool = {}
        self.cur = {}
        for q in ("sync", "gpsimd"):
            self.pool[q] = [f"q_{q}{i}" for i in range(self.NPOOL)]
            for n in self.pool[q]:
                self.cnt[n] = 0
            self.cur[q] = 0

    def stream(self, q):
        self.cur[q] = (self.cur[q] + 1) % self.NPOOL

    def on(self, eng, fn):
        self.ops[eng].append(fn)

    def inc(self, name, amt):
        self.cnt[name] += amt
        return self.cnt[name]

    def wait(self, eng, name, val=None):
        if isinstance(name, tuple):
            name, val = name
        v = self.cnt[name] if val is None else val
        if v <= 0:
            return
        self.ops[eng].append(lambda e, n=name, vv=v: e.wait_ge(self.sems[n], vv))

    def dma(self, q, out, in_):
        sem = self.pool[q][self.cur[q]]
        self.inc(sem, 16)
        self.ops[q].append(lambda e, o=out, i=in_, s_=sem:
                           e.dma_start(out=o, in_=i).then_inc(self.sems[s_], 16))
        return (sem, self.cnt[sem])

    def emit(self, eng, builder, inc=False):
        sem = "e_" + eng
        tgt = self.inc(sem, 1) if inc else None
        def fn(e, b=builder, t=tgt, s_=sem):
            i = b(e)
            if t is not None:
                i.then_inc(self.sems[s_], 1)
        self.on(eng, fn)
        return (sem, self.cnt[sem])

    def mark(self, name):
        self.marks = getattr(self, "marks", {})
        self.marks[name] = ({e: len(self.ops[e]) for e in self.ENGS}, dict(self.cnt))

    def truncate(self, name):
        lens, cnt = self.marks[name]
        for e in self.ENGS:
            self.ops[e] = self.ops[e][:lens[e]]
        self.cnt = dict(cnt)


def build(dbg=(), stop=None, dbg3=None):
    nc = bass.Bass("TRN2", target_bir_lowering=False, debug=False)
    s = Sched()

    ap = {}
    def ein(name, shape, dt):
        ap[name] = nc.dram_tensor(name, shape, dt, kind="ExternalInput").ap()
    ein("x_full", [S, E], F32)
    ein("x_q", [QTOK, E], F32)
    ein("Wq", [E, E], F32)
    ein("Wk", [KVH * D, E], F32)
    ein("Wv", [KVH * D, E], F32)
    ein("Wo", [E, E], F32)
    ein("p1", [H, D, D], F32)
    ein("b1", [D, H], F32)
    ein("p2", [H, D, D], F32)
    ein("b2", [D, H], F32)
    ein("cos_q", [D, QTOK], F32)
    ein("sin_q", [D, QTOK], F32)
    ein("cos_k", [D, S], F32)
    ein("sin_k", [D, S], F32)
    ein("causal", [QSB, 128, S], BF16)
    ein("ident", [128, 128], F32)
    ein("w1q_sh", [ORR, E], I32)
    ein("w1s_sh", [ORR, E // SCB], F32)
    ein("w2q_sh", [ORR, E], I32)
    ein("w2s_sh", [ORR, E // SCB], F32)
    ein("w3q_sh", [E, ORR], I32)
    ein("w3s_sh", [E, ORR // SCB], F32)

    y = nc.dram_tensor("y", [QTOK, E], F32, kind="ExternalOutput").ap()

    dr = {}
    def scr(name, shape, dt):
        dr[name] = nc.dram_tensor(name, shape, dt, kind="Internal")
    scr("xT_d", [E, S], BF16)
    scr("xqT_d", [E, QTOK], BF16)
    scr("wkT_d", [E, KVH * D], BF16)
    scr("wvT_d", [E, KVH * D], BF16)
    scr("woT_d", [E, E], BF16)
    scr("krT_d", [KVH * D, S], BF16)
    scr("v_d", [KVH * 128, NSB * 128], BF16)
    scr("attn_d", [E, QTOK], BF16)
    scr("x2_d", [QTOK, E], BF16)
    scr("ffo_d", [QTOK, E], F32)
    scr("w1T_sh_d", [ORR, E], BF16)
    scr("w2T_sh_d", [ORR, E], BF16)
    scr("w3T_sh_d", [ORR, E], BF16)
    scr("g1_d", [8 * ORR, E], BF16)
    scr("g2_d", [8 * ORR, E], BF16)
    scr("g3_d", [8 * ORR, E], BF16)

    dbg_outs = {}
    dbg_dtypes = {"xT_d": BF16, "xqT_d": BF16, "wkT_d": BF16, "wvT_d": BF16,
                  "woT_d": BF16, "krT_d": BF16, "v_d": BF16, "attn_d": BF16,
                  "x2_d": BF16, "ffo_d": F32, "w1T_sh_d": BF16, "w2T_sh_d": BF16,
                  "w3T_sh_d": BF16, "g1_d": BF16, "g2_d": BF16, "g3_d": BF16}
    for name in dbg:
        shp = [int(v) for v in dr[name].ap().shape]
        dbg_outs[name] = nc.dram_tensor("dbg_" + name, shp, dbg_dtypes[name],
                                        kind="ExternalOutput").ap()
    d3o = {}
    if dbg3 is not None:
        for nm, w, dt3 in [("sim", S, BF16), ("dge", S, BF16), ("deq", S, F16),
                           ("dP", S, F16), ("keep", S, BF16), ("E", S, BF16),
                           ("PnT", S, BF16), ("vg", S, BF16), ("krg", S, BF16),
                           ("attnout", 128, BF16), ("lo", QSB, F32), ("hi", QSB, F32),
                           ("chi", QSB, F32), ("cnt", QSB, F32), ("m", QSB, F32),
                           ("rs", 1, F32), ("rc", 1, F32)]:
            d3o[nm] = nc.dram_tensor("dbg3_" + nm, [128, w], dt3,
                                     kind="ExternalOutput").ap()

    MBASE = 17408

    class Region:
        def __init__(self, base):
            self.cur = base
        def take(self, nbytes):
            off = self.cur
            self.cur += (nbytes + 63) & ~63
            return off

    def nbytes(shape, dt):
        n = 1
        for v in shape[1:]:
            n *= v
        return n * {F32: 4, F16: 2, BF16: 2, I32: 4}[dt]

    LAY = {}
    def lay(region, name, shape, dt):
        LAY[name] = (region.take(nbytes(shape, dt)), shape, dt)

    pers = Region(MBASE)
    lay(pers, "c_ident", [128, 128], BF16)
    lay(pers, "c_b1", [128, H], F32)
    lay(pers, "c_b2", [128, H], F32)
    lay(pers, "ones_col", [128, 1], F32)
    lay(pers, "vflush", [128, 1], F32)
    lay(pers, "sflush", [128, 1], F32)
    trr = Region(pers.cur)
    lay(trr, "tr_nat", [128, 2, E], BF16)
    lay(trr, "tr_stage", [128, 1024], BF16)
    OV = trr.cur

    p0 = Region(OV)
    lay(p0, "dq_codes", [128, E], BF16)
    lay(p0, "dq_u", [128, E], F16)
    lay(p0, "dq_acc", [128, E], F16)
    lay(p0, "dq_w", [128, E], BF16)
    lay(p0, "dq_s", [128, E // SCB], F32)

    p2 = Region(OV)
    lay(p2, "kv_xT", [128, EC * 512], BF16)
    lay(p2, "kv_wc", [128, 2, KVH * 128], BF16)
    lay(p2, "kv_rope", [128, 512], BF16)
    lay(p2, "kv_kr", [128, 2, 512], BF16)
    lay(p2, "kv_t0", [128, 512], BF16)
    lay(p2, "kv_evac", [128, 2, 512], BF16)
    lay(p2, "c_cosk", [128, S], F32)
    lay(p2, "c_sink", [128, S], F32)

    p3 = Region(OV)
    lay(p3, "at_wqnat", [128, 2, E], BF16)
    lay(p3, "at_wqT", [128, E], BF16)
    lay(p3, "at_xq", [128, 2, QTOK], BF16)
    lay(p3, "at_q", [128, QTOK], BF16)
    lay(p3, "at_rope", [128, QTOK], BF16)
    lay(p3, "at_qr", [128, QTOK], BF16)
    lay(p3, "at_p1", [128, 128], BF16)
    lay(p3, "at_p2", [128, 128], BF16)
    lay(p3, "at_s1", [128, S], BF16)
    lay(p3, "at_h1q", [128, QTOK], BF16)
    lay(p3, "at_h1k", [128, S], BF16)
    lay(p3, "at_qh", [128, QTOK], BF16)
    lay(p3, "at_kh", [128, S], BF16)
    lay(p3, "at_krg", [128, S], BF16)
    lay(p3, "at_vg", [128, S], BF16)
    lay(p3, "at_causal", [128, 2, S], BF16)
    lay(p3, "at_sim", [128, SIMW], BF16)
    lay(p3, "at_dump", [128, S], BF16)
    lay(p3, "at_dge", [128, S], BF16)
    lay(p3, "at_deq", [128, S], F16)
    lay(p3, "at_dP", [128, S], F16)
    lay(p3, "at_keep", [128, S], BF16)
    lay(p3, "at_E", [128, S], BF16)
    lay(p3, "at_PnT", [128, S], BF16)
    for nm in ("at_lo", "at_hi", "at_chi", "at_cnt", "at_mid", "at_take", "at_m"):
        lay(p3, nm, [128, QSB], F32)
    lay(p3, "at_rs", [128, 1], F32)
    lay(p3, "at_rc", [128, 1], F32)
    lay(p3, "at_attn", [128, 2, 128], BF16)
    lay(p3, "c_cosq", [128, QTOK], F32)
    lay(p3, "c_sinq", [128, QTOK], F32)

    p4 = Region(OV)
    lay(p4, "wo_lhs", [128, 2, 128], BF16)
    lay(p4, "wo_rhs", [128, 2, E], BF16)
    lay(p4, "wo_x1", [128, E], F32)
    lay(p4, "wo_xorig", [128, 2, E], F32)
    lay(p4, "wo_sq", [128, E], F32)
    for nm in ("ln_sum", "ln_sq", "ln_mean", "ln_var", "ln_rstd", "ln_t0"):
        lay(p4, nm, [128, 1], F32)
    lay(p4, "p4_x2out", [128, E], BF16)

    p5 = Region(OV)
    lay(p5, "ff_x2T", [128, EC * 512], BF16)
    lay(p5, "ff_w1", [128, 2, E], BF16)
    lay(p5, "ff_w2", [128, 2, E], BF16)
    lay(p5, "ff_w3", [128, 2, 512], BF16)
    lay(p5, "ff_silu", [128, 512], BF16)
    lay(p5, "ff_evac", [128, 2, 512], F32)
    lay(p5, "ff_x2row", [128, 2, E], BF16)
    ffh0 = p5.cur
    lay(p5, "ff_h", [128, OB * 512], BF16)
    ln2 = Region(ffh0)
    lay(ln2, "l2_x1", [128, E], F32)
    lay(ln2, "l2_sq", [128, E], F32)
    lay(ln2, "l2_out", [128, E], F32)
    for nm in ("l2_sum", "l2_sq1", "l2_mean", "l2_var", "l2_rstd", "l2_t0"):
        lay(ln2, nm, [128, 1], F32)

    assert max(r.cur for r in (p0, p2, p3, p4, p5)) <= 228000

    ps_specs = [(n, [128, 512], F32) for n in
                ("psA", "psB", "psC", "psD", "psE", "psF", "psG")] + \
               [("psT", [128, 1024], BF16)]
    PSN = ["psA", "psB", "psC", "psD", "psE", "psF", "psG"]

    import contextlib
    stack = contextlib.ExitStack()
    sb = {}
    for name, (off, shape, dt) in LAY.items():
        sb[name] = nc.alloc_sbuf_tensor_at(name, shape, dt, offset=off)
    psb = {}
    for name, shape, dt in ps_specs:
        psb[name] = stack.enter_context(nc.psum_tensor(name, shape, dt))
    for n in list(s.cnt.keys()):
        s.sems[n] = stack.enter_context(nc.semaphore(n))

    def mm(out, lhsT, rhs, start, stop, inc=False):
        return s.emit("tensor", lambda e, o=out, l=lhsT, r=rhs, st=start, sp=stop:
                      e.matmul(o, l, r, start=st, stop=sp), inc=inc)

    def tr(out, in_, inc=False):
        return s.emit("tensor", lambda e, o=out, i=in_:
                      e.transpose(o, i, sb["c_ident"][:]), inc=inc)

    def act(out, in_, func, bias=0.0, scale=1.0, accum=None, inc=False):
        return s.emit("scalar", lambda e, o=out, i=in_, f=func, b=bias, sc=scale,
                      a=accum: e.activation(out=o, in_=i, func=f, bias=b, scale=sc,
                                            accum_out=a), inc=inc)

    def ts(out, in0, s1v, s2v, op0, op1=None, accum=None, inc=False, eng="vector"):
        def fn(e, o=out, i=in0, a=s1v, b=s2v, o0=op0, o1=op1, ac=accum):
            if o1 is None:
                return e.tensor_scalar(out=o, in0=i, scalar1=a, scalar2=b, op0=o0)
            return e.tensor_scalar(out=o, in0=i, scalar1=a, scalar2=b, op0=o0,
                                   op1=o1, accum_out=ac)
        return s.emit(eng, fn, inc=inc)

    def tt(out, in0, in1, op, inc=False, eng="vector"):
        return s.emit(eng, lambda e, o=out, a=in0, b=in1, op_=op:
                      e.tensor_tensor(o, a, b, op_), inc=inc)

    def stt(out, in0, scalar, in1, op0, op1, accum=None, inc=False, eng="vector"):
        return s.emit(eng, lambda e, o=out, a=in0, sc=scalar, b=in1, o0=op0, o1=op1,
                      ac=accum: e.scalar_tensor_tensor(out=o, in0=a, scalar=sc, in1=b,
                                                       op0=o0, op1=o1, accum_out=ac),
                      inc=inc)

    def memset(buf, val, eng="vector", inc=False):
        return s.emit(eng, lambda e, b=buf, v=val: e.memset(b, v), inc=inc)

    def vflush():
        s.emit("vector", lambda e: e.memset(sb["vflush"][:], 0.0), inc=True)

    def vbar():
        vflush()
        s.wait("vector", "e_vector")

    def sflush():
        s.emit("scalar", lambda e: e.activation(out=sb["sflush"][:],
                                                in_=sb["ones_col"][:], func=AF.Copy,
                                                bias=0.0, scale=1.0, accum_out=None),
               inc=True)

    def barrier():
        for eng in Sched.ENGS:
            for semn, v in list(s.cnt.items()):
                if semn == "cc":
                    continue
                if v > 0:
                    s.wait(eng, semn, v)

    s.stream("gpsimd")
    s.dma("gpsimd", sb["c_ident"][:], ap["ident"])
    s.dma("gpsimd", sb["c_b1"][:], ap["b1"])
    s.dma("gpsimd", sb["c_b2"][:], ap["b2"])
    memset(sb["ones_col"][:], 1.0, inc=True)

    def dequant_slab(codes_hbm, scale_hbm, ncols, scale_cols):
        s.wait("gpsimd", "e_vector")
        s.wait("gpsimd", "e_tensor")
        s.stream("gpsimd")
        tok = s.dma("gpsimd", sb["dq_codes"][:, 0:ncols], codes_hbm)
        tok = s.dma("gpsimd", sb["dq_s"][:, 0:scale_cols], scale_hbm)
        s.wait("vector", tok)
        s.wait("vector", "e_tensor")
        memset(sb["dq_acc"][:, 0:ncols], -1.0)
        for k in range(1, 16):
            ts(sb["dq_u"][:, 0:ncols], sb["dq_codes"][:, 0:ncols], float(k), None,
               ALU.is_ge)
            stt(sb["dq_acc"][:, 0:ncols], sb["dq_u"][:, 0:ncols], DELTAS[k - 1],
                sb["dq_acc"][:, 0:ncols], ALU.mult, ALU.add)
        sc_b = sb["dq_s"][:, 0:scale_cols] \
            .rearrange("p (c one) -> p c one", one=1) \
            .to_broadcast([128, scale_cols, SCB])
        acc3 = sb["dq_acc"][:, 0:ncols].rearrange("p (c k) -> p c k", k=SCB)
        w3v = sb["dq_w"][:, 0:ncols].rearrange("p (c k) -> p c k", k=SCB)
        tt(w3v, acc3, sc_b, ALU.mult, inc=True)
        vflush()

    st_tok = [None]

    def transpose_blocks_to(dst_ap_fn, nblocks, src_tile_fn):
        for g in range(ceil_div(nblocks, 8)):
            nblk = min(8, nblocks - g * 8)
            s.wait("tensor", "e_scalar")
            for j in range(nblk):
                tr(psb["psT"][:, j * 128:(j + 1) * 128], src_tile_fn(g * 8 + j),
                   inc=(j == nblk - 1))
            s.wait("scalar", "e_tensor")
            if st_tok[0] is not None:
                s.wait("scalar", st_tok[0])
            act(sb["tr_stage"][:, 0:nblk * 128], psb["psT"][:, 0:nblk * 128], AF.Copy,
                inc=True)
            sflush()
            s.wait("sync", "e_scalar")
            s.stream("sync")
            st_tok[0] = s.dma("sync", dst_ap_fn(g, nblk),
                              sb["tr_stage"][:, 0:nblk * 128])

    for qn, sn, dst in [("w1q_sh", "w1s_sh", "w1T_sh_d"),
                        ("w2q_sh", "w2s_sh", "w2T_sh_d")]:
        for ob in range(O_PER_RANK):
            r0 = ob * 128
            dequant_slab(ap[qn][r0:r0 + 128, :], ap[sn][r0:r0 + 128, :], E, E // SCB)
            s.wait("tensor", "e_vector")
            transpose_blocks_to(
                lambda g, nblk, dst=dst, r0=r0:
                    dr[dst].ap()[r0:r0 + 128, g * 1024:g * 1024 + nblk * 128],
                EC, lambda i: sb["dq_w"][:, i * 128:(i + 1) * 128])
    for eb in range(EC):
        r0 = eb * 128
        dequant_slab(ap["w3q_sh"][r0:r0 + 128, :], ap["w3s_sh"][r0:r0 + 128, :],
                     ORR, ORR // SCB)
        s.wait("tensor", "e_vector")
        transpose_blocks_to(
            lambda g, nblk, r0=r0:
                dr["w3T_sh_d"].ap()[g * 1024:g * 1024 + nblk * 128, r0:r0 + 128]
                .rearrange("(blk p) ee -> p blk ee", p=128),
            O_PER_RANK, lambda i: sb["dq_w"][:, i * 128:(i + 1) * 128])
    s.mark("p0")

    barrier()
    nat_tok = [None, None]

    ld_nat = [None]

    def transpose_to_scratch(src_ap, nrows, dst_name):
        def _ld_nat(rb):
            buf = rb % 2
            if nat_tok[buf] is not None:
                s.wait("gpsimd", "e_tensor", nat_tok[buf])
            s.stream("gpsimd")
            return s.dma("gpsimd", sb["tr_nat"][:, buf, :],
                         src_ap[rb * 128:(rb + 1) * 128, :])
        for rb in range(nrows // 128):
            buf = rb % 2
            if ld_nat[0] is None:
                ld_nat[0] = _ld_nat(rb)
            s.wait("tensor", ld_nat[0])
            ld_nat[0] = _ld_nat(rb + 1) if rb + 1 < nrows // 128 else None
            transpose_blocks_to(
                lambda g, nblk, rb=rb:
                    dr[dst_name].ap()[g * 1024:g * 1024 + nblk * 128,
                                      rb * 128:(rb + 1) * 128]
                    .rearrange("(blk p) r -> p blk r", p=128),
                EC, lambda i, buf=buf: sb["tr_nat"][:, buf, i * 128:(i + 1) * 128])
            nat_tok[buf] = s.cnt["e_tensor"]

    transpose_to_scratch(ap["x_full"], S, "xT_d")
    transpose_to_scratch(ap["x_q"], QTOK, "xqT_d")
    transpose_to_scratch(ap["Wk"], KVH * D, "wkT_d")
    transpose_to_scratch(ap["Wv"], KVH * D, "wvT_d")
    transpose_to_scratch(ap["Wo"], E, "woT_d")
    s.mark("p1")

    barrier()
    s.stream("gpsimd")
    cos_tok = s.dma("gpsimd", sb["c_cosk"][:], ap["cos_k"])
    cos_tok = s.dma("gpsimd", sb["c_sink"][:], ap["sin_k"])
    s.wait("vector", cos_tok)
    wc_tok = [None, None]
    ev_tok = [None, None]
    kr_tok = [None, None]
    for sl in range(S // 512):
        t0 = sl * 512
        s.wait("sync", "e_tensor")
        s.stream("sync")
        for grp4 in range(8):
            xt_tok = s.dma("sync",
                           sb["kv_xT"][:, grp4 * 2048:(grp4 + 1) * 2048]
                           .rearrange("p (ec t) -> p ec t", t=512),
                           dr["xT_d"].ap()[grp4 * 512:(grp4 + 1) * 512, t0:t0 + 512]
                           .rearrange("(ec p) t -> p ec t", p=128))
        s.wait("tensor", xt_tok)
        for pair in range(2):
            def _ld_wv(ec):
                buf = ec % 2
                if wc_tok[buf] is not None:
                    s.wait("sync", "e_tensor", wc_tok[buf])
                s.stream("sync")
                return s.dma("sync", sb["kv_wc"][:, buf, :],
                             dr["wvT_d"].ap()[ec * 128:(ec + 1) * 128, :])
            lv_tok = {0: _ld_wv(0)}
            for ec in range(EC):
                buf = ec % 2
                s.wait("tensor", lv_tok[ec])
                if ec + 1 < EC:
                    lv_tok[ec + 1] = _ld_wv(ec + 1)
                for tb in range(2):
                    tbg = pair * 2 + tb
                    for dh in range(2):
                        mm(psb[PSN[tb * 2 + dh]][:],
                           sb["kv_xT"][:, ec * 512 + tbg * 128:ec * 512 + (tbg + 1) * 128],
                           sb["kv_wc"][:, buf, dh * 512:(dh + 1) * 512],
                           start=(ec == 0), stop=(ec == EC - 1),
                           inc=(tb == 1 and dh == 1))
                wc_tok[buf] = s.cnt["e_tensor"]
            s.wait("scalar", "e_tensor")
            for tb in range(2):
                tbg = pair * 2 + tb
                jb = sl * 4 + tbg
                for dh in range(2):
                    buf = (tb * 2 + dh) % 2
                    if ev_tok[buf] is not None:
                        s.wait("scalar", ev_tok[buf])
                    act(sb["kv_evac"][:, buf, :], psb[PSN[tb * 2 + dh]][:], AF.Copy,
                        inc=True)
                    sflush()
                    s.wait("sync", "e_scalar")
                    s.stream("sync")
                    for kl in range(4):
                        kvh_i = dh * 4 + kl
                        ev_tok[buf] = s.dma(
                            "sync",
                            dr["v_d"].ap()[kvh_i * 128:(kvh_i + 1) * 128,
                                           jb * 128:(jb + 1) * 128],
                            sb["kv_evac"][:, buf, kl * 128:(kl + 1) * 128])
            s.wait("tensor", "e_scalar")
        for pair in range(2):
            def _ld_wk(ec):
                buf = ec % 2
                if wc_tok[buf] is not None:
                    s.wait("sync", "e_tensor", wc_tok[buf])
                s.stream("sync")
                return s.dma("sync", sb["kv_wc"][:, buf, :],
                             dr["wkT_d"].ap()[ec * 128:(ec + 1) * 128, :])
            lk_tok = {0: _ld_wk(0)}
            for ec in range(EC):
                buf = ec % 2
                s.wait("tensor", lk_tok[ec])
                if ec + 1 < EC:
                    lk_tok[ec + 1] = _ld_wk(ec + 1)
                for kk in range(4):
                    kvh_i = pair * 4 + kk
                    mm(psb[PSN[kk]][:],
                       sb["kv_wc"][:, buf, kvh_i * 128:(kvh_i + 1) * 128],
                       sb["kv_xT"][:, ec * 512:(ec + 1) * 512],
                       start=(ec == 0), stop=(ec == EC - 1), inc=(kk == 3))
                wc_tok[buf] = s.cnt["e_tensor"]
            s.wait("vector", "e_tensor")
            for kk in range(4):
                kvh_i = pair * 4 + kk
                kb = kk % 2
                if kr_tok[kb] is not None:
                    s.wait("vector", kr_tok[kb])
                ts(sb["kv_kr"][:, kb, :], psb[PSN[kk]][:], 1.0, None, ALU.mult)
                ts(sb["kv_rope"][0:64, :], sb["kv_kr"][64:128, kb, :], -1.0, None,
                   ALU.mult)
                ts(sb["kv_rope"][64:128, :], sb["kv_kr"][0:64, kb, :], 1.0, None,
                   ALU.mult)
                tt(sb["kv_t0"][:], sb["kv_kr"][:, kb, :], sb["c_cosk"][:, t0:t0 + 512],
                   ALU.mult)
                tt(sb["kv_rope"][:], sb["kv_rope"][:], sb["c_sink"][:, t0:t0 + 512],
                   ALU.mult)
                tt(sb["kv_kr"][:, kb, :], sb["kv_t0"][:], sb["kv_rope"][:], ALU.add,
                   inc=True)
                vflush()
                s.wait("sync", "e_vector")
                s.stream("sync")
                kr_tok[kb] = s.dma("sync",
                                   dr["krT_d"].ap()[kvh_i * 128:(kvh_i + 1) * 128,
                                                    t0:t0 + 512],
                                   sb["kv_kr"][:, kb, :])
            s.wait("tensor", "e_vector")
    barrier()
    for src, dst in [("w1T_sh_d", "g1_d"), ("w2T_sh_d", "g2_d"), ("w3T_sh_d", "g3_d")]:
        s.inc("cc", 1)
        def fn(e, src=src, dst=dst):
            e.collective_compute(
                "AllGather", ALU.bypass,
                replica_groups=[[0, 1, 2, 3, 4, 5, 6, 7]],
                ins=[dr[src].ap().opt()],
                outs=[dr[dst].ap().opt()],
            ).then_inc(s.sems["cc"], 1)
        s.on("gpsimd", fn)
    s.mark("p2")

    barrier()
    s.stream("gpsimd")
    cq_tok = s.dma("gpsimd", sb["c_cosq"][:], ap["cos_q"])
    cq_tok = s.dma("gpsimd", sb["c_sinq"][:], ap["sin_q"])
    s.wait("vector", cq_tok)
    xq_tok = [None, None]
    ca_tok = [None, None]
    an_tok = [None, None]
    wqn_tok = [None, None]

    def _ld_wq(h):
        wb = h % 2
        if wqn_tok[wb] is not None:
            s.wait("gpsimd", "e_tensor", wqn_tok[wb])
        s.stream("gpsimd")
        return s.dma("gpsimd", sb["at_wqnat"][:, wb, :],
                     ap["Wq"][h * 128:(h + 1) * 128, :])
    wql_tok = {0: _ld_wq(0)}
    for h in range(H):
        kv = h // GRP
        wb = h % 2
        s.wait("gpsimd", "e_tensor")
        s.stream("gpsimd")
        wq_tok = s.dma("gpsimd", sb["at_p1"][:], ap["p1"][h, :, :])
        wq_tok = s.dma("gpsimd", sb["at_p2"][:], ap["p2"][h, :, :])
        if h % GRP == 0:
            s.wait("sync", "e_tensor")
            s.stream("sync")
            kvg_tok = s.dma("sync", sb["at_krg"][:],
                            dr["krT_d"].ap()[kv * 128:(kv + 1) * 128, :])
            kvg_tok = s.dma("sync", sb["at_vg"][:],
                            dr["v_d"].ap()[kv * 128:(kv + 1) * 128, :])
            s.wait("tensor", kvg_tok)
            s.wait("vector", kvg_tok)
        s.wait("tensor", wq_tok)
        s.wait("tensor", wql_tok[h])
        if h + 1 < H:
            wql_tok[h + 1] = _ld_wq(h + 1)
        for g in range(4):
            s.wait("tensor", "e_scalar")
            for j in range(8):
                ec = g * 8 + j
                tr(psb["psT"][:, j * 128:(j + 1) * 128],
                   sb["at_wqnat"][:, wb, ec * 128:(ec + 1) * 128], inc=(j == 7))
            s.wait("scalar", "e_tensor")
            act(sb["at_wqT"][:, g * 1024:(g + 1) * 1024], psb["psT"][:, 0:1024],
                AF.Copy, inc=True)
        wqn_tok[wb] = s.cnt["e_tensor"]
        s.wait("tensor", "e_scalar")
        def _ld_xq(ec):
            buf = ec % 2
            if xq_tok[buf] is not None:
                s.wait("sync", "e_tensor", xq_tok[buf])
            s.stream("sync")
            return s.dma("sync", sb["at_xq"][:, buf, :],
                         dr["xqT_d"].ap()[ec * 128:(ec + 1) * 128, :])
        lq_tok = {0: _ld_xq(0)}
        for ec in range(EC):
            buf = ec % 2
            s.wait("tensor", lq_tok[ec])
            if ec + 1 < EC:
                lq_tok[ec + 1] = _ld_xq(ec + 1)
            for nh in range(2):
                mm(psb[PSN[nh]][:], sb["at_wqT"][:, ec * 128:(ec + 1) * 128],
                   sb["at_xq"][:, buf, nh * 512:(nh + 1) * 512],
                   start=(ec == 0), stop=(ec == EC - 1), inc=(nh == 1))
            xq_tok[buf] = s.cnt["e_tensor"]
        s.wait("vector", "e_tensor")
        ts(sb["at_q"][:, 0:512], psb["psA"][:], 1.0, None, ALU.mult)
        ts(sb["at_q"][:, 512:1024], psb["psB"][:], 1.0, None, ALU.mult)
        ts(sb["at_rope"][0:64, :], sb["at_q"][64:128, :], -1.0, None, ALU.mult)
        ts(sb["at_rope"][64:128, :], sb["at_q"][0:64, :], 1.0, None, ALU.mult)
        tt(sb["at_qr"][:], sb["at_q"][:], sb["c_cosq"][:], ALU.mult)
        tt(sb["at_rope"][:], sb["at_rope"][:], sb["c_sinq"][:], ALU.mult)
        tt(sb["at_qr"][:], sb["at_qr"][:], sb["at_rope"][:], ALU.add, inc=True)
        vflush()
        s.wait("tensor", "e_vector")
        for nh in range(2):
            mm(psb[PSN[nh]][:], sb["at_p1"][:], sb["at_qr"][:, nh * 512:(nh + 1) * 512],
               start=True, stop=True, inc=(nh == 1))
        s.wait("scalar", "e_tensor")
        s.wait("scalar", "e_vector")
        act(sb["at_s1"][:, 0:512], psb["psA"][:], AF.Silu, bias=sb["c_b1"][:, h:h + 1])
        act(sb["at_s1"][:, 512:1024], psb["psB"][:], AF.Silu,
            bias=sb["c_b1"][:, h:h + 1], inc=True)
        sflush()
        s.wait("vector", "e_scalar")
        tt(sb["at_h1q"][:], sb["at_s1"][:, 0:QTOK], sb["at_qr"][:], ALU.add, inc=True)
        vflush()
        s.wait("tensor", "e_vector")
        for nh in range(2):
            pst = psb[PSN[nh]]
            mm(pst[:], sb["at_p2"][:], sb["at_h1q"][:, nh * 512:(nh + 1) * 512],
               start=True, stop=False)
            mm(pst[:], sb["c_ident"][:], sb["at_h1q"][:, nh * 512:(nh + 1) * 512],
               start=False, stop=True, inc=(nh == 1))
        s.wait("scalar", "e_tensor")
        act(sb["at_qh"][:, 0:512], psb["psA"][:], AF.Sign, bias=sb["c_b2"][:, h:h + 1])
        act(sb["at_qh"][:, 512:1024], psb["psB"][:], AF.Sign,
            bias=sb["c_b2"][:, h:h + 1], inc=True)
        sflush()
        for nh in range(4):
            pst = psb[PSN[2 + nh % 2]]
            if nh >= 2:
                s.wait("tensor", "e_scalar")
            mm(pst[:], sb["at_p1"][:], sb["at_krg"][:, nh * 512:(nh + 1) * 512],
               start=True, stop=True, inc=True)
            s.wait("scalar", "e_tensor")
            act(sb["at_s1"][:, nh * 512:(nh + 1) * 512], pst[:], AF.Silu,
                bias=sb["c_b1"][:, h:h + 1], inc=True)
        sflush()
        s.wait("vector", "e_scalar")
        tt(sb["at_h1k"][:], sb["at_s1"][:], sb["at_krg"][:], ALU.add, inc=True)
        vflush()
        s.wait("tensor", "e_vector")
        for nh in range(4):
            pst = psb[PSN[2 + nh % 2]]
            if nh >= 2:
                s.wait("tensor", "e_scalar")
            mm(pst[:], sb["at_p2"][:], sb["at_h1k"][:, nh * 512:(nh + 1) * 512],
               start=True, stop=False)
            mm(pst[:], sb["c_ident"][:], sb["at_h1k"][:, nh * 512:(nh + 1) * 512],
               start=False, stop=True, inc=True)
            s.wait("scalar", "e_tensor")
            act(sb["at_kh"][:, nh * 512:(nh + 1) * 512], pst[:], AF.Sign,
                bias=sb["c_b2"][:, h:h + 1], inc=True)
        sflush()
        s.wait("tensor", "e_scalar")

        for sbi in range(QSB):
            jext = SLOT_JB[sbi] * 128
            off = SLOT_OFF[sbi]
            cbuf = sbi % 2
            if ca_tok[cbuf] is not None:
                s.wait("gpsimd", "e_vector", ca_tok[cbuf])
            s.stream("gpsimd")
            ctok = s.dma("gpsimd", sb["at_causal"][:, cbuf, 0:jext],
                         ap["causal"][sbi, :, 0:jext])
            qtile = sb["at_qh"][:, sbi * 128:(sbi + 1) * 128]
            for c in range(ceil_div(jext, 512)):
                w = min(512, jext - c * 512)
                pst = psb[PSN[4 + c % 2]]
                s.wait("tensor", "e_vector")
                mm(pst[:, 0:w], qtile, sb["at_kh"][:, c * 512:c * 512 + w],
                   start=True, stop=True, inc=True)
                s.wait("vector", "e_tensor")
                if c == 0:
                    s.wait("vector", ctok)
                tt(sb["at_sim"][:, off + c * 512:off + c * 512 + w], pst[:, 0:w],
                   sb["at_causal"][:, cbuf, c * 512:c * 512 + w], ALU.add, inc=True)
            vflush()
            ca_tok[cbuf] = s.cnt["e_vector"]
        sim_done_tok = s.cnt["e_vector"]

        memset(sb["at_mid"][:], -130.0 + 256.0, inc=True)
        vbar()
        for it in range(8):
            step = float(1 << (8 - it))
            for sbi in range(QSB):
                jext = SLOT_JB[sbi] * 128
                off = SLOT_OFF[sbi]
                ts(sb["at_dump"][:, 0:jext], sb["at_sim"][:, off:off + jext],
                   sb["at_mid"][:, sbi:sbi + 1], None, ALU.is_ge, op1=ALU.add,
                   accum=sb["at_cnt"][:, sbi:sbi + 1], inc=(sbi == QSB - 1))
            vbar()
            ts(sb["at_take"][:], sb["at_cnt"][:], float(NUM_REMAIN), step,
               ALU.is_ge, op1=ALU.mult, inc=True)
            vbar()
            stt(sb["at_mid"][:], sb["at_take"][:], -step / 2.0, sb["at_mid"][:],
                ALU.add, ALU.add, inc=True)
            vbar()
        ts(sb["at_hi"][:], sb["at_mid"][:], 0.0, None, ALU.add)
        ts(sb["at_lo"][:], sb["at_mid"][:], -1.0, None, ALU.add, inc=True)
        vbar()
        for sbi in range(QSB):
            jext = SLOT_JB[sbi] * 128
            off = SLOT_OFF[sbi]
            ts(sb["at_dump"][:, 0:jext], sb["at_sim"][:, off:off + jext],
               sb["at_hi"][:, sbi:sbi + 1], None, ALU.is_ge, op1=ALU.add,
               accum=sb["at_chi"][:, sbi:sbi + 1], inc=(sbi == QSB - 1))
        vbar()
        ts(sb["at_m"][:], sb["at_chi"][:], -1.0, float(NUM_REMAIN), ALU.mult,
           op1=ALU.add, inc=True)
        vbar()

        for sbi in range(QSB):
            jext = SLOT_JB[sbi] * 128
            njb = SLOT_JB[sbi]
            off = SLOT_OFF[sbi]
            ts(sb["at_deq"][:, 0:jext], sb["at_sim"][:, off:off + jext],
               sb["at_lo"][:, sbi:sbi + 1], None, ALU.is_equal)
            ts(sb["at_dge"][:, 0:jext], sb["at_sim"][:, off:off + jext],
               sb["at_hi"][:, sbi:sbi + 1], None, ALU.is_ge, inc=True)
            vbar()
            s.emit("vector", lambda e, je=jext: e.tensor_tensor_scan(
                out=sb["at_dP"][:, 0:je],
                data0=sb["ones_col"][:].to_broadcast([128, je]),
                data1=sb["at_deq"][:, 0:je], initial=0.0, op0=ALU.mult, op1=ALU.add),
                inc=True)
            vbar()
            stt(sb["at_deq"][:, 0:jext], sb["at_dP"][:, 0:jext],
                sb["at_m"][:, sbi:sbi + 1], sb["at_deq"][:, 0:jext],
                ALU.is_le, ALU.mult, inc=True)
            vbar()
            tt(sb["at_keep"][:, 0:jext], sb["at_dge"][:, 0:jext],
               sb["at_deq"][:, 0:jext], ALU.add, inc=True)
            vbar()
            if sbi == 0:
                s.wait("tensor", "e_vector", sim_done_tok)
            for c in range(ceil_div(jext, 512)):
                w = min(512, jext - c * 512)
                pst = psb[PSN[4 + c % 2]]
                s.wait("tensor", "e_scalar")
                mm(pst[:, 0:w], sb["at_qr"][:, sbi * 128:(sbi + 1) * 128],
                   sb["at_krg"][:, c * 512:c * 512 + w], start=True, stop=True,
                   inc=True)
                s.wait("scalar", "e_tensor")
                act(sb["at_E"][:, c * 512:c * 512 + w], pst[:, 0:w], AF.Exp,
                    scale=INV_SQRT_D, inc=True)
            sflush()
            s.wait("vector", "e_scalar")
            tt(sb["at_E"][:, 0:jext], sb["at_E"][:, 0:jext],
               sb["at_keep"][:, 0:jext], ALU.mult, inc=True)
            vbar()
            ts(sb["at_dump"][:, 0:jext], sb["at_E"][:, 0:jext], 1.0, None, ALU.mult,
               op1=ALU.add, accum=sb["at_rs"][:], inc=True)
            vbar()
            s.emit("vector", lambda e: e.reciprocal(out=sb["at_rc"][:],
                                                    in_=sb["at_rs"][:]), inc=True)
            vbar()
            ts(sb["at_E"][:, 0:jext], sb["at_E"][:, 0:jext], sb["at_rc"][:], None,
               ALU.mult, inc=True)
            vbar()
            if dbg3 is not None and (h, sbi) == dbg3:
                s.stream("sync")
                s.dma("sync", d3o["sim"][:, 0:jext], sb["at_sim"][:, off:off + jext])
                s.dma("sync", d3o["dge"][:, 0:jext], sb["at_dge"][:, 0:jext])
                s.dma("sync", d3o["deq"][:, 0:jext], sb["at_deq"][:, 0:jext])
                s.dma("sync", d3o["dP"][:, 0:jext], sb["at_dP"][:, 0:jext])
                s.dma("sync", d3o["keep"][:, 0:jext], sb["at_keep"][:, 0:jext])
                tok = s.dma("sync", d3o["E"][:, 0:jext], sb["at_E"][:, 0:jext])
                for nm in ("lo", "hi", "chi", "cnt", "m"):
                    s.dma("sync", d3o[nm], sb["at_" + nm][:])
                s.dma("sync", d3o["rs"], sb["at_rs"][:])
                s.dma("sync", d3o["rc"], sb["at_rc"][:])
                s.dma("sync", d3o["vg"], sb["at_vg"][:])
                tok = s.dma("sync", d3o["krg"], sb["at_krg"][:])
                s.wait("vector", tok)
            s.wait("tensor", "e_vector")
            for g in range(ceil_div(njb, 8)):
                nblk = min(8, njb - g * 8)
                s.wait("tensor", "e_scalar")
                for j in range(nblk):
                    jb = g * 8 + j
                    tr(psb["psT"][:, j * 128:(j + 1) * 128],
                       sb["at_E"][:, jb * 128:(jb + 1) * 128], inc=(j == nblk - 1))
                s.wait("scalar", "e_tensor")
                act(sb["at_PnT"][:, g * 1024:g * 1024 + nblk * 128],
                    psb["psT"][:, 0:nblk * 128], AF.Copy, inc=True)
            sflush()
            s.wait("tensor", "e_scalar")
            for jb in range(njb):
                mm(psb["psG"][:, 0:128], sb["at_vg"][:, jb * 128:(jb + 1) * 128],
                   sb["at_PnT"][:, jb * 128:(jb + 1) * 128],
                   start=(jb == 0), stop=(jb == njb - 1), inc=(jb == njb - 1))
            abuf = sbi % 2
            if an_tok[abuf] is not None:
                s.wait("scalar", an_tok[abuf])
            s.wait("scalar", "e_tensor")
            act(sb["at_attn"][:, abuf, :], psb["psG"][:, 0:128], AF.Copy, inc=True)
            sflush()
            s.wait("sync", "e_scalar")
            s.stream("sync")
            an_tok[abuf] = s.dma("sync",
                                 dr["attn_d"].ap()[h * 128:(h + 1) * 128,
                                                   sbi * 128:(sbi + 1) * 128],
                                 sb["at_attn"][:, abuf, :])
            if dbg3 is not None and (h, sbi) == dbg3:
                tok = s.dma("sync", d3o["PnT"][:, 0:jext], sb["at_PnT"][:, 0:jext])
                tok = s.dma("sync", d3o["attnout"], sb["at_attn"][:, abuf, :])
                s.wait("scalar", tok)
            s.wait("tensor", "e_scalar")
    s.mark("p3")

    barrier()

    def emit_ln(x1buf, outbuf, bufs):
        sqb, sumc, sqc, meanc, varc, rstdc, t0c = bufs
        s.wait("vector", "e_scalar")
        ts(sqb, x1buf, 1.0, None, ALU.mult, op1=ALU.add, accum=sumc, inc=True)
        vflush()
        s.wait("scalar", "e_vector")
        act(sqb, x1buf, AF.Square, accum=sqc, inc=True)
        sflush()
        s.wait("vector", "e_scalar")
        ts(meanc, sumc, 1.0 / E, None, ALU.mult, inc=True)
        vbar()
        ts(varc, sqc, 1.0 / E, None, ALU.mult, inc=True)
        vflush()
        s.wait("scalar", "e_vector")
        act(t0c, meanc, AF.Square, inc=True)
        sflush()
        s.wait("vector", "e_scalar")
        stt(varc, t0c, -1.0, varc, ALU.mult, ALU.add, inc=True)
        vbar()
        ts(varc, varc, 1e-5, None, ALU.add, inc=True)
        vflush()
        s.wait("scalar", "e_vector")
        act(t0c, varc, AF.Sqrt, inc=True)
        sflush()
        s.wait("vector", "e_scalar")
        s.emit("vector", lambda e, r=rstdc, t=t0c: e.reciprocal(out=r, in_=t),
               inc=True)
        vbar()
        ts(outbuf, x1buf, meanc, rstdc, ALU.subtract, op1=ALU.mult, inc=True)
        vbar()

    LN1SET = (sb["wo_sq"][:, 0:E], sb["ln_sum"][:], sb["ln_sq"][:], sb["ln_mean"][:],
              sb["ln_var"][:], sb["ln_rstd"][:], sb["ln_t0"][:])
    LN2SET = (sb["l2_sq"][:, 0:E], sb["l2_sum"][:], sb["l2_sq1"][:], sb["l2_mean"][:],
              sb["l2_var"][:], sb["l2_rstd"][:], sb["l2_t0"][:])

    lhs_tok = [None, None]
    xo_tok = [None, None]
    x2st_tok = None
    for sbi in range(QSB):
        xb = sbi % 2
        if xo_tok[xb] is not None:
            s.wait("gpsimd", "e_vector", xo_tok[xb])
        s.stream("gpsimd")
        xotok = s.dma("gpsimd", sb["wo_xorig"][:, xb, :],
                      ap["x_q"][sbi * 128:(sbi + 1) * 128, :])
        for ep in range(2):
            def _ld_wo(hd, sbi=sbi):
                buf = hd % 2
                if lhs_tok[buf] is not None:
                    s.wait("sync", "e_tensor", lhs_tok[buf])
                s.stream("sync")
                s.dma("sync", sb["wo_lhs"][:, buf, :],
                      dr["attn_d"].ap()[hd * 128:(hd + 1) * 128,
                                        sbi * 128:(sbi + 1) * 128])
                return s.dma("sync", sb["wo_rhs"][:, buf, :],
                             dr["woT_d"].ap()[hd * 128:(hd + 1) * 128, :])
            lw_tok = {0: _ld_wo(0)}
            for hd in range(EC):
                buf = hd % 2
                s.wait("tensor", lw_tok[hd])
                if hd + 1 < EC:
                    lw_tok[hd + 1] = _ld_wo(hd + 1)
                for e4 in range(4):
                    ech = ep * 4 + e4
                    mm(psb[PSN[e4]][:], sb["wo_lhs"][:, buf, :],
                       sb["wo_rhs"][:, buf, ech * 512:(ech + 1) * 512],
                       start=(hd == 0), stop=(hd == EC - 1), inc=(e4 == 3))
                lhs_tok[buf] = s.cnt["e_tensor"]
            s.wait("vector", "e_tensor")
            if ep == 0:
                s.wait("vector", xotok)
            for e4 in range(4):
                ech = ep * 4 + e4
                tt(sb["wo_x1"][:, ech * 512:(ech + 1) * 512], psb[PSN[e4]][:],
                   sb["wo_xorig"][:, xb, ech * 512:(ech + 1) * 512], ALU.add,
                   inc=(e4 == 3))
            vflush()
            s.wait("tensor", "e_vector")
        xo_tok[xb] = s.cnt["e_vector"]
        if x2st_tok is not None:
            s.wait("vector", x2st_tok)
        emit_ln(sb["wo_x1"][:, 0:E], sb["p4_x2out"][:, 0:E], LN1SET)
        s.wait("sync", "e_vector")
        s.stream("sync")
        x2st_tok = s.dma("sync", dr["x2_d"].ap()[sbi * 128:(sbi + 1) * 128, :],
                         sb["p4_x2out"][:, 0:E])
    s.mark("p4")

    barrier()
    w_tok = [None, None]
    w3_tok = [None, None]
    x2_tok = [None, None]
    for ch in range(2):
        t0 = ch * 512
        for tb in range(4):
            xb = tb % 2
            if x2_tok[xb] is not None:
                s.wait("sync", "e_tensor", x2_tok[xb])
            s.stream("sync")
            xtok = s.dma("sync", sb["ff_x2row"][:, xb, 0:E],
                         dr["x2_d"].ap()[t0 + tb * 128:t0 + (tb + 1) * 128, :])
            s.wait("tensor", xtok)
            for g in range(4):
                s.wait("tensor", "e_scalar")
                for j in range(8):
                    ec = g * 8 + j
                    tr(psb["psT"][:, j * 128:(j + 1) * 128],
                       sb["ff_x2row"][:, xb, ec * 128:(ec + 1) * 128], inc=(j == 7))
                s.wait("scalar", "e_tensor")
                for j in range(8):
                    ec = g * 8 + j
                    act(sb["ff_x2T"][:, ec * 512 + tb * 128:ec * 512 + (tb + 1) * 128],
                        psb["psT"][:, j * 128:(j + 1) * 128], AF.Copy, inc=(j == 7))
            x2_tok[xb] = s.cnt["e_tensor"]
        s.wait("tensor", "e_scalar")
        if ch == 0:
            s.wait("sync", "cc")
        def _ld_w12(g):
            r, l = _o_rank_of_block(g)
            row0 = r * ORR + l * 128
            buf = g % 2
            if w_tok[buf] is not None:
                s.wait("sync", "e_tensor", w_tok[buf])
            s.stream("sync")
            s.dma("sync", sb["ff_w1"][:, buf, :], dr["g1_d"].ap()[row0:row0 + 128, :])
            return s.dma("sync", sb["ff_w2"][:, buf, :],
                         dr["g2_d"].ap()[row0:row0 + 128, :])
        ld_tok = {0: _ld_w12(0)}
        for g in range(OB):
            buf = g % 2
            s.wait("tensor", ld_tok[g])
            if g + 1 < OB:
                ld_tok[g + 1] = _ld_w12(g + 1)
            for ec in range(EC):
                mm(psb["psA"][:], sb["ff_w1"][:, buf, ec * 128:(ec + 1) * 128],
                   sb["ff_x2T"][:, ec * 512:(ec + 1) * 512],
                   start=(ec == 0), stop=(ec == EC - 1))
                mm(psb["psB"][:], sb["ff_w2"][:, buf, ec * 128:(ec + 1) * 128],
                   sb["ff_x2T"][:, ec * 512:(ec + 1) * 512],
                   start=(ec == 0), stop=(ec == EC - 1), inc=(ec == EC - 1))
            w_tok[buf] = s.cnt["e_tensor"]
            s.wait("scalar", "e_tensor")
            s.wait("scalar", "e_vector")
            act(sb["ff_silu"][:], psb["psA"][:], AF.Silu, inc=True)
            sflush()
            s.wait("vector", "e_scalar")
            stt(sb["ff_h"][:, g * 512:(g + 1) * 512], psb["psB"][:], 1.0,
                sb["ff_silu"][:], ALU.mult, ALU.mult, inc=True)
            vflush()
            s.wait("tensor", "e_vector")
        ev2_tok = [None, None]
        for ech in range(8):
            e0 = ech * 512
            def _ld_w3(g, e0=e0):
                r, l = _o_rank_of_block(g)
                row0 = r * ORR + l * 128
                buf = g % 2
                if w3_tok[buf] is not None:
                    s.wait("sync", "e_tensor", w3_tok[buf])
                s.stream("sync")
                return s.dma("sync", sb["ff_w3"][:, buf, :],
                             dr["g3_d"].ap()[row0:row0 + 128, e0:e0 + 512])
            l3_tok = {0: _ld_w3(0)}
            for g in range(OB):
                buf = g % 2
                s.wait("tensor", l3_tok[g])
                if g + 1 < OB:
                    l3_tok[g + 1] = _ld_w3(g + 1)
                for sub in range(4):
                    mm(psb[PSN[sub]][:],
                       sb["ff_h"][:, g * 512 + sub * 128:g * 512 + (sub + 1) * 128],
                       sb["ff_w3"][:, buf, :],
                       start=(g == 0), stop=(g == OB - 1), inc=(sub == 3))
                w3_tok[buf] = s.cnt["e_tensor"]
            s.wait("vector", "e_tensor")
            for sub in range(4):
                vb = sub % 2
                if ev2_tok[vb] is not None:
                    s.wait("vector", ev2_tok[vb])
                ts(sb["ff_evac"][:, vb, :], psb[PSN[sub]][:], 1.0, None, ALU.mult,
                   inc=True)
                vflush()
                s.wait("sync", "e_vector")
                s.stream("sync")
                ev2_tok[vb] = s.dma("sync",
                                    dr["ffo_d"].ap()[t0 + sub * 128:t0 + (sub + 1) * 128,
                                                     e0:e0 + 512],
                                    sb["ff_evac"][:, vb, :])
            s.wait("tensor", "e_vector")
    barrier()
    for tb in range(QSB):
        xb = tb % 2
        s.wait("sync", "e_vector")
        s.stream("sync")
        x2tok = s.dma("sync", sb["ff_x2row"][:, xb, 0:E],
                      dr["x2_d"].ap()[tb * 128:(tb + 1) * 128, :])
        s.wait("gpsimd", "e_vector")
        s.stream("gpsimd")
        fftok = s.dma("gpsimd", sb["l2_x1"][:, 0:E],
                      dr["ffo_d"].ap()[tb * 128:(tb + 1) * 128, :])
        s.wait("vector", x2tok)
        s.wait("vector", fftok)
        tt(sb["l2_x1"][:, 0:E], sb["l2_x1"][:, 0:E], sb["ff_x2row"][:, xb, 0:E],
           ALU.add, inc=True)
        vflush()
        emit_ln(sb["l2_x1"][:, 0:E], sb["l2_out"][:, 0:E], LN2SET)
        s.wait("sync", "e_vector")
        s.stream("sync")
        ytok = s.dma("sync", y[tb * 128:(tb + 1) * 128, :], sb["l2_out"][:, 0:E])
        s.wait("vector", ytok)
    s.mark("p5")
    if stop is not None:
        s.truncate(stop)

    if dbg:
        barrier()
        s.wait("gpsimd", "cc")
        for name in dbg:
            n_r = int(dr[name].ap().shape[0])
            step = max(1, (1 << 22) // max(1, int(dr[name].ap().shape[1])))
            r0 = 0
            s.stream("gpsimd")
            while r0 < n_r:
                r1 = min(n_r, r0 + step)
                s.dma("gpsimd", dbg_outs[name][r0:r1, :], dr[name].ap()[r0:r1, :])
                r0 = r1

    for semn, v in list(s.cnt.items()):
        if v > 0:
            s.wait("gpsimd", semn, v)

    with stack:
        with nc.Block() as block:
            @block.sync
            def _(eng):
                for fn in s.ops["sync"]:
                    fn(eng)

            @block.tensor
            def _(eng):
                for fn in s.ops["tensor"]:
                    fn(eng)

            @block.vector
            def _(eng):
                for fn in s.ops["vector"]:
                    fn(eng)

            @block.scalar
            def _(eng):
                for fn in s.ops["scalar"]:
                    fn(eng)

            @block.gpsimd
            def _(eng):
                for fn in s.ops["gpsimd"]:
                    fn(eng)

    return nc


def _prep_inputs(core, L0_attn_hidden_states, L0_attn_Wq, L0_attn_Wk, L0_attn_Wv,
                 L0_attn_Wo, L0_attn_hash_proj1, L0_attn_hash_bias1,
                 L0_attn_hash_proj2, L0_attn_hash_bias2, L0_ffn_w1_q, L0_ffn_w1_scale,
                 L0_ffn_w2_q, L0_ffn_w2_scale, L0_ffn_w3_q, L0_ffn_w3_scale,
                 consts):
    b, half = core // 2, core % 2
    slots = PAIR_SLOTS[half]
    xb = np.ascontiguousarray(L0_attn_hidden_states[b])
    rows = np.concatenate([np.arange(g * 128, (g + 1) * 128) for g in slots])
    d = {
        "x_full": xb,
        "x_q": np.ascontiguousarray(xb[rows]),
        "Wq": L0_attn_Wq, "Wk": L0_attn_Wk, "Wv": L0_attn_Wv, "Wo": L0_attn_Wo,
        "p1": L0_attn_hash_proj1, "p2": L0_attn_hash_proj2,
        "b1": np.ascontiguousarray(L0_attn_hash_bias1.reshape(H, D).T),
        "b2": np.ascontiguousarray(L0_attn_hash_bias2.reshape(H, D).T),
        "cos_q": np.ascontiguousarray(consts["cos"][:, rows]),
        "sin_q": np.ascontiguousarray(consts["sin"][:, rows]),
        "cos_k": consts["cos"], "sin_k": consts["sin"],
        "causal": consts["causal"][half],
        "ident": consts["ident"],
    }
    r = core
    if r < 6:
        gblk = list(range(11 * r, 11 * r + 11))
    else:
        g0 = 66 + 10 * (r - 6)
        gblk = list(range(g0, g0 + 10)) + [None]
    def rows_of(blks, W, axis, blkrows=128):
        out = []
        for g in blks:
            if g is None:
                shp = list(W.shape)
                shp[axis] = blkrows
                out.append(np.zeros(shp, W.dtype))
            else:
                sl = [slice(None)] * W.ndim
                sl[axis] = slice(g * blkrows, (g + 1) * blkrows)
                out.append(W[tuple(sl)])
        return np.ascontiguousarray(np.concatenate(out, axis=axis))
    d["w1q_sh"] = rows_of(gblk, L0_ffn_w1_q, 0)
    d["w2q_sh"] = rows_of(gblk, L0_ffn_w2_q, 0)
    d["w3q_sh"] = rows_of(gblk, L0_ffn_w3_q, 1)
    d["w1s_sh"] = rows_of(gblk, L0_ffn_w1_scale, 0)
    d["w2s_sh"] = rows_of(gblk, L0_ffn_w2_scale, 0)
    d["w3s_sh"] = rows_of(gblk, L0_ffn_w3_scale, 1, blkrows=2)
    return d


def _make_consts():
    import ml_dtypes
    inv_freq = 1.0 / 10000.0 ** (np.arange(0, D, 2, dtype=np.float64) / D)
    t = np.arange(S, dtype=np.float64)
    freqs = np.outer(t, inv_freq)
    emb = np.concatenate([freqs, freqs], -1)
    cos = np.ascontiguousarray(np.cos(emb).T.astype(np.float32))
    sin = np.ascontiguousarray(np.sin(emb).T.astype(np.float32))
    causal = np.zeros((2, QSB, 128, S), dtype=ml_dtypes.bfloat16)
    jj = np.arange(S)
    for half in range(2):
        for sl, g in enumerate(PAIR_SLOTS[half]):
            srow = g * 128 + np.arange(128)
            m = np.where(jj[None, :] <= srow[:, None], 0.0, NEGC)
            causal[half, sl] = m.astype(ml_dtypes.bfloat16)
    ident = np.eye(128, dtype=np.float32)
    return {"cos": cos, "sin": sin, "causal": causal, "ident": ident}


_NC_CACHE = {}


def kernel(**inputs):
    if "main" not in _NC_CACHE:
        _NC_CACHE["main"] = build()
    nc = _NC_CACHE["main"]
    consts = _make_consts()
    in_maps = [_prep_inputs(c, consts=consts, **inputs) for c in range(8)]
    res = run_bass_kernel_spmd(nc, in_maps, core_ids=list(range(8)))
    out = np.zeros((B, S, E), dtype=np.float32)
    for c in range(8):
        b, half = c // 2, c % 2
        yc = res.results[c]["y"]
        for sl, g in enumerate(PAIR_SLOTS[half]):
            out[b, g * 128:(g + 1) * 128, :] = yc[sl * 128:(sl + 1) * 128, :]
    return out
